# version 03
# baseline (speedup 1.0000x reference)
import os
import numpy as np
from contextlib import ExitStack

import concourse.bass as bass
import concourse.mybir as mybir
from concourse.bass_utils import run_bass_kernel_spmd
from concourse.alu_op_type import AluOpType as OP

F32 = mybir.dt.float32
BF16 = mybir.dt.bfloat16
I32 = mybir.dt.int32
AF = mybir.ActivationFunctionType
AX = mybir.AxisListType

D = 1024
PT = D // 128
NH, HD, NF = 16, 64, 256
FF0 = 2048
NE = 8
K_TAPS = 32
H1 = 4096
RAD, HD1 = 64, 256


class EC:
    ENGS = ("t", "v", "s", "g", "y")

    def __init__(self, nc, es):
        self.nc = nc
        self.es = es
        self.plan = {e: [] for e in self.ENGS}
        self.cnt = {e: 0 for e in self.ENGS}
        self.sem = {e: es.enter_context(nc.semaphore(f"sem_{e}")) for e in self.ENGS}
        self.dsem = {}
        self.dcnt = {}
        self.ccsem = es.enter_context(nc.semaphore("ccsem"))
        self.cccnt = 0
        self.wm = {e: {} for e in self.ENGS}
        self.pread = None

    def _waits(self, e, deps):
        agg = {}
        for dep in deps:
            if dep is None:
                continue
            kind, val = dep
            if kind == e and val >= self.cnt[e] + 1:
                raise RuntimeError(f"self-dep in future: {e} {val} > {self.cnt[e]}")
            if val > agg.get(kind, 0):
                agg[kind] = val
        out = []
        for kind, val in agg.items():
            if val <= 0 or self.wm[e].get(kind, 0) >= val:
                continue
            self.wm[e][kind] = val
            if kind in self.ENGS:
                out.append((self.sem[kind], val))
            elif kind.startswith("D:"):
                out.append((self.dsem[kind[2:]], val))
            elif kind == "cc":
                out.append((self.ccsem, val))
            else:
                raise KeyError(kind)
        return out

    def op(self, e, f, deps=(), n_inc=1, psum=False):
        deps = list(deps)
        if psum and self.pread is not None:
            deps.append(self.pread)
        waits = self._waits(e, deps)
        self.cnt[e] += n_inc
        sem = self.sem[e]

        def step(eng, waits=waits, f=f, sem=sem, n_inc=n_inc):
            for s, v in waits:
                eng.wait_ge(s, v)
            f(eng).then_inc(sem, n_inc)
        self.plan[e].append(step)
        tok = (e, self.cnt[e])
        if psum:
            self.pread = tok
        return tok

    def raw(self, e, f, deps=()):
        waits = self._waits(e, list(deps))

        def step(eng, waits=waits, f=f):
            for s, v in waits:
                eng.wait_ge(s, v)
            f(eng)
        self.plan[e].append(step)

    def dma(self, e, key, f, deps=()):
        if key not in self.dsem:
            self.dsem[key] = self.es.enter_context(self.nc.semaphore("dk_" + key))
            self.dcnt[key] = 0
        deps = list(deps)
        if e == "g" and self.dcnt[key] >= 16:
            deps.append(("D:" + key, self.dcnt[key]))
        waits = self._waits(e, deps)
        self.dcnt[key] += 16
        sem = self.dsem[key]

        def step(eng, waits=waits, f=f, sem=sem):
            for s, v in waits:
                eng.wait_ge(s, v)
            f(eng).then_inc(sem, 16)
        self.plan[e].append(step)
        return ("D:" + key, self.dcnt[key])

    def cc(self, f, deps=()):
        waits = self._waits("g", list(deps))
        self.cccnt += 1
        sem = self.ccsem

        def step(eng, waits=waits, f=f, sem=sem):
            for s, v in waits:
                eng.wait_ge(s, v)
            f(eng).then_inc(sem, 1)
        self.plan["g"].append(step)
        return ("cc", self.cccnt)

    def bar(self):
        toks = [(e, self.cnt[e]) for e in self.ENGS if self.cnt[e] > 0]
        toks += [("D:" + k, v) for k, v in self.dcnt.items() if v > 0]
        if self.cccnt:
            toks.append(("cc", self.cccnt))
        return toks


def build_program(Tc, n_cores=8, enable=(1, 2, 3, 4, 5, 6)):
    TCW = 512 if Tc % 512 == 0 else 128
    NTC = Tc // TCW
    NKW = TCW // 128
    nc = bass.Bass("TRN2", target_bir_lowering=False)
    gsz = n_cores // 2
    groups = [list(range(gsz)), list(range(gsz, n_cores))]

    x_in = nc.dram_tensor("x_in", [Tc, D], F32, kind="ExternalInput").ap()
    y_out = nc.dram_tensor("y_out", [Tc, D], F32, kind="ExternalOutput").ap()
    consts = nc.dram_tensor("consts", [128, 160], F32, kind="ExternalInput").ap()

    W = {}
    def win(name, shape):
        W[name] = nc.dram_tensor(name, shape, F32, kind="ExternalInput").ap()
    VPP = [128, PT]
    for nm, sh in [("Wq", [D, D]), ("bq", VPP), ("Wk", [D, D]), ("bk", VPP),
                   ("Wv", [D, D]), ("bv", VPP), ("bvR", [1, D]), ("Wo0", [D, D]), ("bo0", VPP),
                   ("projT", [NH, HD, NF]),
                   ("Wp1", [D, 128]), ("bp1", [128]), ("Wp2", [128, NE]), ("bp2", [NE]),
                   ("W1", [NE, D, FF0]), ("b1T", [NE, 128, FF0 // 128]),
                   ("W2", [NE, FF0, D]), ("b2T", [NE, 128, D // 128]),
                   ("Wof", [D, D]), ("bof", VPP),
                   ("ln_w", VPP), ("ln_b", VPP), ("A_log", [D, 16]), ("B_mat", [D, 16]),
                   ("C_mat", [D, 16]), ("D_vec", VPP), ("out_wT", [D, D]), ("out_b", VPP),
                   ("ln2_w", VPP), ("ln2_b", VPP), ("ffn1_wT", [D, H1]), ("ffn1_b", [128, H1 // 128]),
                   ("ffn2_wT", [H1, D]), ("ffn2_b", VPP),
                   ("W_base", [D, D]), ("b_base", VPP),
                   ("Wd1", [NE, D, HD1]), ("bd1", [NE, 128, HD1 // 128]),
                   ("Wd2T", [NE, D, HD1]), ("bd2", [NE, 128, D // 128]),
                   ("Wa_down", [NE, D, RAD]), ("ba_down", [NE, RAD]),
                   ("Wa_up", [NE, RAD, D]), ("ba_up", [NE, 128, D // 128])]:
        win(nm, sh)

    arena = {}
    off = 0
    for nm, n in [("Wq", D * D), ("Wk", D * D), ("Wv", D * D), ("Wo0", D * D),
                  ("Wof", D * D), ("out_wT", D * D), ("W_base", D * D),
                  ("ffn1_wT", D * H1), ("ffn2_wT", H1 * D),
                  ("Wd1", NE * D * HD1), ("Wd2T", NE * D * HD1),
                  ("projT", NH * HD * NF),
                  ("W1s1", D * FF0), ("W2s1", FF0 * D),
                  ("W1s3", D * FF0), ("W2s3", FF0 * D)]:
        arena[nm] = off
        off += n
    wscr = nc.dram_tensor("wscr", [off], BF16).ap()
    cc_in = nc.dram_tensor("cc_in", [128 * 2112], F32).ap()
    cc_out = nc.dram_tensor("cc_out", [128 * 2112], F32).ap()
    agr_in = nc.dram_tensor("agr_in", [1024], F32).ap()
    ag_out = nc.dram_tensor("ag_out", [gsz * 1024], F32).ap()
    h_scr = nc.dram_tensor("h_scr", [2 * D * Tc], BF16).ap()
    hal_in = nc.dram_tensor("hal_in", [D * 32], BF16).ap()
    hal_out = nc.dram_tensor("hal_out", [gsz * D * 32], BF16).ap()

    es = ExitStack()
    with es:
        ec = EC(nc, es)

        def sb(name, shape, dt=F32):
            return es.enter_context(nc.sbuf_tensor(name, shape, dt))

        X = sb("X", [128, PT * Tc], F32)
        LNDT = F32 if os.environ.get("K_LNF32") else BF16
        mean_t = sb("mean_t", [128, Tc], LNDT)
        rstd_t = sb("rstd_t", [128, Tc], LNDT)
        poolA = sb("poolA", [128, 16 * 512], BF16)
        poolB = sb("poolB", [128, 8 * 512], BF16)
        poolC = sb("poolC", [128, 8 * 512], BF16)
        stage = sb("stage", [128, 2 * D], F32)
        sw = sb("sw", [128, 3 * D], BF16)
        dbuf = sb("dbuf", [128, 512], F32)
        hbuf = sb("hbuf", [128, 512], F32)
        varb = sb("varb", [128, 512], F32)
        msq = sb("msq", [128, 512], F32)
        ones_b = sb("ones_b", [128, 128], BF16)
        ones_f = sb("ones_f", [128, 128], F32)
        ident_f = sb("ident_f", [128, 128], F32)
        eps_t = sb("eps_t", [128, 1], F32)
        eps8_t = sb("eps8_t", [128, 1], F32)
        ident_b = sb("ident_b", [128, 128], BF16)
        projs = sb("projs", [128, 8 * NF], BF16)
        bones_f = sb("bones_f", [128, 128], F32)
        gbuf = sb("gbuf", [128, 512], F32)

        biasp = sb("biasp", [128, 108], F32)
        smallp = sb("smallp", [128, 216], F32)
        idxi = sb("idxi", [1, 1], I32)
        iot8 = sb("iot8", [1, 8], F32)
        pb = [es.enter_context(nc.psum_tensor(f"pb{i}", [128, 512], F32)) for i in range(8)]

        selreg = es.enter_context(nc.gpsimd.register("selreg"))
        offreg = es.enter_context(nc.gpsimd.register("offreg"))

        block = es.enter_context(nc.Block())

        ec.op("g", lambda g: g.memset(ones_b[:], 1.0))
        ec.op("g", lambda g: g.memset(ones_f[:], 1.0))
        ec.op("g", lambda g: g.memset(eps_t[:], 1e-5))
        ec.op("g", lambda g: g.memset(eps8_t[:], 1e-8))
        t_bz = ec.op("g", lambda g: g.memset(bones_f[:], 0.0))
        t_b1 = ec.op("g", lambda g: g.memset(bones_f[0:64, 0:64], 1.0), deps=[t_bz])
        ec.op("g", lambda g: g.memset(bones_f[64:128, 64:128], 1.0), deps=[t_b1])
        d_c = ec.dma("y", "mcst", lambda y: y.dma_start(
            ident_f[:], bass.AP(consts.tensor, 0, [[160, 128], [1, 128]])))
        d_ci = ec.dma("y", "mcio", lambda y: y.dma_start(
            iot8[0:1, 0:8], bass.AP(consts.tensor, 128, [[160, 1], [1, 8]])))
        d_cm = ec.dma("y", "mcmk", lambda y: y.dma_start(
            biasp[:, 104:108], bass.AP(consts.tensor, 141, [[160, 128], [1, 4]])))
        t_id = ec.op("v", lambda v: v.tensor_copy(ident_b[:], ident_f[:]), deps=[d_c])
        t_idb = t_id
        tok_init = ec.bar()

        def cast_to_arena(nm_dst, hbm, row0, nrows, ncols, deps):
            if os.environ.get("K_NOCAST"):
                return list(deps)
            toks = []
            base = arena[nm_dst]
            step = max(1, (2 ** 21) // ncols)
            for r0 in range(0, nrows, step):
                nr = min(step, nrows - r0)
                toks.append(ec.dma("g", "aro_" + nm_dst,
                            lambda g, r0=r0, nr=nr: g.dma_start(
                    bass.AP(wscr.tensor, base + r0 * ncols, [[ncols, nr], [1, ncols]]),
                    bass.AP(hbm.tensor, (row0 + r0) * ncols, [[ncols, nr], [1, ncols]])),
                    deps=list(deps)))
            return toks

        NT128 = Tc // 128
        bank_rd = {}
        stage_rd = None
        for t in range(NT128 if not os.environ.get("K_NOIN") else 0):
            stg = stage[:, 0:D]
            dep_prev = [stage_rd] if stage_rd else list(tok_init)
            dtok = ec.dma("y", "io0", lambda y, stg=stg, t=t: y.dma_start(
                stg, x_in[t * 128:(t + 1) * 128, :]), deps=dep_prev)
            for p in range(PT):
                bi = p % 4
                bank = pb[bi]
                ttok = ec.op("t", lambda te, stg=stg, p=p, bank=bank: te.transpose(
                    bank[:, 0:128], stg[:, p * 128:(p + 1) * 128], ident_f[:]),
                    deps=[dtok, bank_rd.get(bi), t_id])
                vtok = ec.op("v", lambda v, p=p, t=t, bank=bank: v.tensor_copy(
                    X[:, p * Tc + t * 128: p * Tc + (t + 1) * 128], bank[:, 0:128]),
                    deps=[ttok], psum=True)
                bank_rd[bi] = vtok
                if p == PT - 1:
                    stage_rd = ttok
        tok_x = ec.bar()

        arena_tok = {}
        arena_tok["Wq"] = cast_to_arena("Wq", W["Wq"], 0, D, D, tok_init)
        arena_tok["Wk"] = cast_to_arena("Wk", W["Wk"], 0, D, D, tok_init)
        arena_tok["Wv"] = cast_to_arena("Wv", W["Wv"], 0, D, D, tok_init)
        arena_tok["Wo0"] = cast_to_arena("Wo0", W["Wo0"], 0, D, D, tok_init)
        arena_tok["Wof"] = cast_to_arena("Wof", W["Wof"], 0, D, D, tok_init)
        arena_tok["out_wT"] = cast_to_arena("out_wT", W["out_wT"], 0, D, D, tok_init)
        arena_tok["W_base"] = cast_to_arena("W_base", W["W_base"], 0, D, D, tok_init)
        arena_tok["ffn1_wT"] = cast_to_arena("ffn1_wT", W["ffn1_wT"], 0, D, H1, tok_init)
        arena_tok["ffn2_wT"] = cast_to_arena("ffn2_wT", W["ffn2_wT"], 0, H1, D, tok_init)
        arena_tok["Wd1"] = cast_to_arena("Wd1", W["Wd1"], 0, NE * D, HD1, tok_init)
        arena_tok["Wd2T"] = cast_to_arena("Wd2T", W["Wd2T"], 0, NE * D, HD1, tok_init)
        arena_tok["projT"] = cast_to_arena("projT", W["projT"], 0, NH * HD, NF, tok_init)

        def emit_ln(deps):
            lnv = int(os.environ.get("K_LN", "8"))
            toks = []
            for tc in range(NTC):
                c0 = tc * TCW
                pA, pB = pb[4], pb[5]
                sq_toks = []
                for p in range(PT):
                    sq_toks.append(ec.op("s", lambda s, p=p, c0=c0: s.activation(
                        poolA[:, p * 512: p * 512 + TCW], X[:, p * Tc + c0: p * Tc + c0 + TCW],
                        AF.Square, bias=0.0, scale=1.0), deps=deps + [("t", ec.cnt["t"])]))
                if lnv <= 1:
                    continue
                mm = mm2 = None
                for p in range(PT):
                    mm = ec.op("t", lambda te, p=p, c0=c0, pA=pA: te.matmul(
                        pA[:, 0:TCW], ones_f[:, 0:128], X[:, p * Tc + c0: p * Tc + c0 + TCW],
                        start=(p == 0), stop=(p == PT - 1)),
                        deps=deps + ([("v", ec.cnt["v"]), ("s", ec.cnt["s"])] if p == 0 else []))
                if lnv >= 3:
                    for p in range(PT):
                        mm2 = ec.op("t", lambda te, p=p, pB=pB: te.matmul(
                            pB[:, 0:TCW], ones_b[:, 0:128], poolA[:, p * 512: p * 512 + TCW],
                            start=(p == 0), stop=(p == PT - 1)),
                            deps=[sq_toks[p]] + ([("v", ec.cnt["v"])] if p == 0 else []))
                if lnv <= 3:
                    continue
                t_mean = ec.op("v", lambda v, c0=c0, pA=pA: v.tensor_scalar(
                    mean_t[:, c0:c0 + TCW], pA[:, 0:TCW], 1.0 / D, None, OP.mult, OP.bypass),
                    deps=[mm], psum=True)
                if lnv <= 4:
                    continue
                t_msq = ec.op("s", lambda s, pA=pA: s.activation(
                    msq[:, 0:TCW], pA[:, 0:TCW], AF.Square, bias=0.0, scale=1.0 / D),
                    deps=[mm] + ([("t", ec.cnt["t"])] if os.environ.get("K_MSQFIX") else []), psum=True)
                if lnv <= 5:
                    continue
                t_var = ec.op("v", lambda v, pB=pB: v.scalar_tensor_tensor(
                    varb[:, 0:TCW], pB[:, 0:TCW], 1.0 / D, msq[:, 0:TCW], OP.mult, OP.subtract),
                    deps=[mm2, t_msq, t_mean], psum=True)
                if lnv <= 6:
                    continue
                t_ln = ec.op("s", lambda s: s.activation(
                    msq[:, 0:TCW], varb[:, 0:TCW], AF.Ln, bias=eps_t[:], scale=1.0),
                    deps=[t_var])
                if lnv <= 7:
                    continue
                t_r = ec.op("s", lambda s, c0=c0: s.activation(
                    rstd_t[:, c0:c0 + TCW], msq[:, 0:TCW], AF.Exp, bias=0.0, scale=-0.5),
                    deps=[t_ln])
                toks += [t_mean, t_r]
            return toks

        def emit_h_chunk(tc, p, dep, add_to_x=0.0, dst=None, dst_col=0):
            c0 = tc * TCW
            if dst is None:
                dst, dst_col = poolB, p * 512
            t_d = ec.op("v", lambda v, p=p, c0=c0: v.tensor_tensor(
                dbuf[:, 0:TCW], X[:, p * Tc + c0:p * Tc + c0 + TCW], mean_t[:, c0:c0 + TCW],
                OP.subtract), deps=list(dep) + [("v", ec.cnt["v"])])
            if add_to_x:
                t_h32 = ec.op("v", lambda v, c0=c0: v.tensor_tensor(
                    hbuf[:, 0:TCW], dbuf[:, 0:TCW], rstd_t[:, c0:c0 + TCW], OP.mult), deps=[t_d])
                t_h = ec.op("v", lambda v, dst=dst, dst_col=dst_col: v.tensor_copy(
                    dst[:, dst_col:dst_col + TCW], hbuf[:, 0:TCW]), deps=[t_h32])
                t_x = ec.op("v", lambda v, p=p, c0=c0, a=add_to_x: v.scalar_tensor_tensor(
                    X[:, p * Tc + c0:p * Tc + c0 + TCW], hbuf[:, 0:TCW], a,
                    X[:, p * Tc + c0:p * Tc + c0 + TCW], OP.mult, OP.add), deps=[t_h32])
                return t_h, t_x
            t_h = ec.op("v", lambda v, c0=c0, dst=dst, dst_col=dst_col: v.tensor_tensor(
                dst[:, dst_col:dst_col + TCW], dbuf[:, 0:TCW], rstd_t[:, c0:c0 + TCW], OP.mult),
                deps=[t_d])
            return t_h, None

        class WStream:
            NSLOT = 3

            def __init__(self):
                self.i = 0
                self.cons = {}

            def load(self, arena_off, ncols=D, row_stride=None, extra_deps=(), colblock=None):
                slot = self.i % self.NSLOT
                self.i += 1
                dep = [self.cons.get(slot)] if slot in self.cons else []
                if colblock is not None:
                    trs, bc = colblock
                    tok = ec.dma("y", f"w{slot}",
                                 lambda y, slot=slot, arena_off=arena_off, trs=trs, bc=bc: y.dma_start(
                        sw[:, slot * D: slot * D + PT * bc],
                        bass.AP(wscr.tensor, arena_off, [[trs, 128], [128 * trs, PT], [1, bc]])),
                        deps=dep + list(extra_deps))
                    return slot, tok
                rs = row_stride if row_stride is not None else ncols
                tok = ec.dma("y", f"w{slot}",
                             lambda y, slot=slot, arena_off=arena_off, ncols=ncols, rs=rs: y.dma_start(
                    sw[:, slot * D: slot * D + ncols],
                    bass.AP(wscr.tensor, arena_off, [[rs, 128], [1, ncols]])),
                    deps=dep + list(extra_deps))
                return slot, tok

            def done(self, slot, tok):
                self.cons[slot] = tok

        ws = WStream()

        def emit_l0ffn(tag):
            arena_w1 = arena["W1s1" if tag == 1 else "W1s3"]
            arena_w2 = arena["W2s1" if tag == 1 else "W2s3"]
            ln_toks = emit_ln(ec.bar())
            bar0 = ec.bar()
            hcol = []
            for p in range(PT):
                t_d = ec.op("v", lambda v, p=p: v.tensor_tensor(
                    smallp[:, 64 + p:64 + p + 1], X[:, p * Tc + Tc - 1: p * Tc + Tc],
                    mean_t[:, Tc - 1:Tc], OP.subtract), deps=bar0)
                hcol.append(ec.op("v", lambda v, p=p: v.tensor_tensor(
                    smallp[:, 72 + p:72 + p + 1], smallp[:, 64 + p:64 + p + 1],
                    rstd_t[:, Tc - 1:Tc], OP.mult), deps=[t_d]))
            d1 = ec.dma("g", "ccio", lambda g: g.dma_start(
                bass.AP(agr_in.tensor, 0, [[PT, 128], [1, PT]]), smallp[:, 72:72 + PT]), deps=hcol)
            cctok = ec.cc(lambda g: g.collective_compute(
                "AllGather", OP.bypass, replica_groups=groups,
                ins=[bass.AP(agr_in.tensor, 0, [[1, 1024]]).opt()],
                outs=[bass.AP(ag_out.tensor, 0, [[1, gsz * 1024]]).opt()]), deps=[d1])
            d2 = ec.dma("g", "ccio", lambda g: g.dma_start(
                smallp[:, 80:80 + PT],
                bass.AP(ag_out.tensor, (gsz - 1) * 1024, [[PT, 128], [1, PT]])), deps=[cctok])
            dw1 = []
            for k in range(8):
                dw1.append(ec.dma("y", "mwp1", lambda y, k=k: y.dma_start(
                    stage[:, k * 128:(k + 1) * 128],
                    bass.AP(W["Wp1"].tensor, k * 128 * 128, [[128, 128], [1, 128]])),
                    deps=bar0))
            dW2 = ec.dma("y", "mwp2", lambda y: y.dma_start(smallp[:, 88:88 + NE], W["Wp2"]), deps=bar0)
            dB1 = ec.dma("y", "mbp1", lambda y: y.dma_start(
                biasp[:, 0:1], bass.AP(W["bp1"].tensor, 0, [[1, 128], [1, 1]])), deps=bar0)
            dB2 = ec.dma("y", "mbp2", lambda y: y.dma_start(
                msq[0:1, 200:200 + NE], bass.AP(W["bp2"].tensor, 0, [[0, 1], [1, NE]])), deps=bar0)
            mmr = None
            for k in range(8):
                mmr = ec.op("t", lambda te, k=k: te.matmul(
                    pb[6][:, 0:1], stage[:, k * 128:(k + 1) * 128], smallp[:, 80 + k:80 + k + 1],
                    start=(k == 0), stop=(k == 7)),
                    deps=[d2] + dw1 + [("s", ec.cnt["s"]), ("v", ec.cnt["v"])])
            t_g1 = ec.op("s", lambda s: s.activation(
                smallp[:, 100:101], pb[6][:, 0:1], AF.Gelu, bias=biasp[:, 0:1], scale=1.0),
                deps=[mmr, dB1], psum=True)
            mm2r = ec.op("t", lambda te: te.matmul(
                pb[7][0:1, 0:NE], smallp[:, 100:101], smallp[:, 88:88 + NE], start=True, stop=True),
                deps=[t_g1, dW2])
            t_lg = ec.op("v", lambda v: v.tensor_tensor(
                msq[0:1, 192:192 + NE], pb[7][0:1, 0:NE], msq[0:1, 200:200 + NE], OP.add), deps=[mm2r, dB2], psum=True)
            t_mx = ec.op("v", lambda v: v.tensor_reduce(
                msq[0:1, 208:209], msq[0:1, 192:192 + NE], axis=AX.X, op=OP.max), deps=[t_lg])
            t_oh = ec.op("v", lambda v: v.tensor_scalar(
                msq[0:1, 216:216 + NE], msq[0:1, 192:192 + NE], msq[0:1, 208:209], None, OP.is_ge, OP.bypass),
                deps=[t_mx])
            t_ix0 = ec.op("v", lambda v: v.tensor_tensor(
                msq[0:1, 216:216 + NE], msq[0:1, 216:216 + NE], iot8[0:1, 0:8], OP.mult),
                deps=[t_oh, d_ci])
            t_ix = ec.op("v", lambda v: v.tensor_scalar(
                varb[0:1, 0:NE], msq[0:1, 216:216 + NE], 1.0, 0.0, OP.mult, OP.add,
                accum_out=msq[0:1, 224:225]), deps=[t_ix0])
            t_ii = ec.op("v", lambda v: v.tensor_copy(idxi[:], msq[0:1, 224:225]), deps=[t_ix])
            ec.raw("g", lambda g: g.reg_load(selreg, idxi[0:1, 0:1]), deps=[t_ii])
            t_selld = ("v", ec.cnt["v"])

            gkey = f"gw{tag}"
            gdeps = ec._waits("g", bar0 + [t_selld])

            def _gather_branches(g, gdeps=gdeps):
                for s_, v_ in gdeps:
                    g.wait_ge(s_, v_)
                for e in range(NE):
                    with g.If_eq(selreg, e):
                        g.dma_start(
                            bass.AP(wscr.tensor, arena_w1, [[FF0, D], [1, FF0]]),
                            bass.AP(W["W1"].tensor, e * D * FF0, [[FF0, D], [1, FF0]])
                        ).then_inc(ec.dsem[gkey], 16)
                        g.dma_start(
                            bass.AP(wscr.tensor, arena_w2, [[D, FF0], [1, D]]),
                            bass.AP(W["W2"].tensor, e * FF0 * D, [[D, FF0], [1, D]])
                        ).then_inc(ec.dsem[gkey], 16)
                        g.dma_start(biasp[:, 0:16],
                                    bass.AP(W["b1T"].tensor, e * FF0, [[16, 128], [1, 16]])
                                    ).then_inc(ec.dsem[gkey], 16)
                        g.dma_start(biasp[:, 16:24],
                                    bass.AP(W["b2T"].tensor, e * D, [[8, 128], [1, 8]])
                                    ).then_inc(ec.dsem[gkey], 16)

            if gkey not in ec.dsem:
                ec.dsem[gkey] = es.enter_context(nc.semaphore("dk_" + gkey))
                ec.dcnt[gkey] = 0
            ec.dcnt[gkey] += 64
            ec.plan["g"].append(_gather_branches)
            gath = [("D:" + gkey, ec.dcnt[gkey])]
            d_b1 = d_b2 = gath[0]
            d_bo = ec.dma("y", "mbo", lambda y: y.dma_start(
                biasp[:, 24:32], W["bof"]), deps=bar0)
            t_bo = ec.op("v", lambda v: v.tensor_scalar(
                biasp[:, 32:40], biasp[:, 24:32], 0.15, None, OP.mult, OP.bypass), deps=[d_bo])

            wof_deps = arena_tok["Wof"]
            for tc in range(NTC):
                h_toks = []
                for p in range(PT):
                    th, _ = emit_h_chunk(tc, p, ln_toks, add_to_x=0.5)
                    h_toks.append(th)
                g_toks = []
                for half in range(2):
                    mm_last = [None] * 8
                    for k in range(PT):
                        slot, wtok = ws.load(arena_w1 + k * 128 * FF0 + half * 1024, row_stride=FF0, extra_deps=gath)
                        for m in range(8):
                            mm_last[m] = ec.op("t", lambda te, k=k, m=m, slot=slot: te.matmul(
                                pb[m][:, 0:TCW], sw[:, slot * D + m * 128: slot * D + (m + 1) * 128],
                                poolB[:, k * 512: k * 512 + TCW],
                                start=(k == 0), stop=(k == PT - 1)),
                                deps=[wtok, h_toks[k]] +
                                     ([("s", ec.cnt["s"])] if k == 0 and m == 0 else []))
                        ws.done(slot, mm_last[7])
                    for m in range(8):
                        mc = half * 8 + m
                        g_toks.append(ec.op("s", lambda s, m=m, mc=mc: s.activation(
                            poolA[:, mc * 512: mc * 512 + TCW], pb[m][:, 0:TCW], AF.Gelu,
                            bias=biasp[:, mc:mc + 1], scale=1.0),
                            deps=[mm_last[m], d_b1], psum=True))
                mm_last = [None] * 8
                for k2 in range(16):
                    slot, wtok = ws.load(arena_w2 + k2 * 128 * D, extra_deps=gath)
                    for mo in range(8):
                        mm_last[mo] = ec.op("t", lambda te, k2=k2, mo=mo, slot=slot: te.matmul(
                            pb[mo][:, 0:TCW], sw[:, slot * D + mo * 128: slot * D + (mo + 1) * 128],
                            poolA[:, k2 * 512: k2 * 512 + TCW],
                            start=(k2 == 0), stop=(k2 == 15)),
                            deps=[wtok, g_toks[k2]] +
                                 ([("s", ec.cnt["s"]), ("v", ec.cnt["v"])] if k2 == 0 and mo == 0 else []))
                    ws.done(slot, mm_last[7])
                r_toks = []
                for mo in range(8):
                    r_toks.append(ec.op("s", lambda s, mo=mo: s.activation(
                        poolC[:, mo * 512: mo * 512 + TCW], pb[mo][:, 0:TCW], AF.Identity,
                        bias=biasp[:, 16 + mo:16 + mo + 1], scale=1.0),
                        deps=[mm_last[mo], d_b2], psum=True))
                mm_last = [None] * 8
                for k3 in range(PT):
                    slot, wtok = ws.load(arena["Wof"] + k3 * 128 * D, extra_deps=wof_deps)
                    for m3 in range(8):
                        mm_last[m3] = ec.op("t", lambda te, k3=k3, m3=m3, slot=slot: te.matmul(
                            pb[m3][:, 0:TCW], sw[:, slot * D + m3 * 128: slot * D + (m3 + 1) * 128],
                            poolC[:, k3 * 512: k3 * 512 + TCW],
                            start=(k3 == 0), stop=(k3 == PT - 1)),
                            deps=[wtok, r_toks[k3]] +
                                 ([("s", ec.cnt["s"]), ("v", ec.cnt["v"])] if k3 == 0 and m3 == 0 else []))
                    ws.done(slot, mm_last[7])
                for m3 in range(8):
                    c0 = tc * TCW
                    t_u = ec.op("v", lambda v, m3=m3: v.tensor_scalar(
                        dbuf[:, 0:TCW], pb[m3][:, 0:TCW], biasp[:, 24 + m3:24 + m3 + 1], 0.15,
                        OP.add, OP.mult), deps=[mm_last[m3], d_bo, ("v", ec.cnt["v"])], psum=True)
                    ec.op("v", lambda v, m3=m3, c0=c0: v.tensor_tensor(
                        X[:, m3 * Tc + c0: m3 * Tc + c0 + TCW],
                        X[:, m3 * Tc + c0: m3 * Tc + c0 + TCW], dbuf[:, 0:TCW], OP.add),
                        deps=[t_u])


        def emit_l0attn():
            lvl = int(os.environ.get("K_STOP", "99"))
            Tglob = 4 * Tc
            if lvl == 0:
                return
            if lvl != 11:
                ln_toks = emit_ln(ec.bar())
            else:
                ln_toks = ec.bar()
            bar0 = ec.bar()
            if lvl != 12:
                for h in range(NH):
                    ec.dma("y", "mproj", lambda y, h=h: y.dma_start(
                        projs[(h % 2) * 64:(h % 2) * 64 + 64, (h // 2) * NF:(h // 2) * NF + NF],
                        bass.AP(wscr.tensor, arena["projT"] + h * HD * NF, [[NF, 64], [1, NF]])),
                        deps=arena_tok["projT"] + (bar0 if h == 0 else []))
            dproj = ("D:mproj", ec.dcnt["mproj"]) if lvl != 12 else None
            d_bq = ec.dma("y", "mbq", lambda y: y.dma_start(biasp[:, 40:48], W["bq"]), deps=bar0)
            d_bk = ec.dma("y", "mbk", lambda y: y.dma_start(biasp[:, 48:56], W["bk"]), deps=bar0)
            d_bv = ec.dma("y", "mbv", lambda y: y.dma_start(biasp[:, 56:64], W["bv"]), deps=bar0)
            d_bo0 = ec.dma("y", "mbo0", lambda y: y.dma_start(biasp[:, 64:72], W["bo0"]), deps=bar0)
            if lvl in (11, 12):
                return

            def proj_norm_chunk(tc, wname, bcol, h_toks, bdep):
                kn_toks = [None] * PT
                for m in range(PT):
                    slot, wtok = ws.load(arena[wname] + m * 128, extra_deps=arena_tok[wname],
                                         colblock=(D, 128))
                    bank = pb[m % 2]
                    mmk = None
                    for k in range(PT):
                        mmk = ec.op("t", lambda te, k=k, bank=bank, slot=slot: te.matmul(
                            bank[:, 0:TCW], sw[:, slot * D + k * 128: slot * D + (k + 1) * 128],
                            poolB[:, k * 512: k * 512 + TCW], start=(k == 0), stop=(k == PT - 1)),
                            deps=[wtok, h_toks[k]] +
                                 ([("s", ec.cnt["s"]), ("v", ec.cnt["v"])] if k == 0 else []))
                    ws.done(slot, mmk)
                    t_sq = ec.op("s", lambda s, m=m, bank=bank: s.activation(
                        stage[:, 0:TCW], bank[:, 0:TCW], AF.Square,
                        bias=biasp[:, bcol + m:bcol + m + 1], scale=1.0),
                        deps=[mmk, bdep, ("t", ec.cnt["t"]), ("v", ec.cnt["v"])], psum=True)
                    mmn = ec.op("t", lambda te: te.matmul(
                        pb[2][:, 0:TCW], bones_f[:], stage[:, 0:TCW], start=True, stop=True),
                        deps=[t_sq, ("s", ec.cnt["s"])])
                    t_r1 = ec.op("s", lambda s: s.activation(
                        msq[:, 0:TCW], pb[2][:, 0:TCW], AF.Sqrt, bias=0.0, scale=1.0),
                        deps=[mmn, ("s", ec.cnt["s"])], psum=True)
                    t_r2 = ec.op("s", lambda s: s.activation(
                        msq[:, 0:TCW], msq[:, 0:TCW], AF.Ln, bias=eps8_t[:], scale=1.0), deps=[t_r1])
                    t_r3 = ec.op("s", lambda s: s.activation(
                        varb[:, 0:TCW], msq[:, 0:TCW], AF.Exp, bias=0.0, scale=-1.0), deps=[t_r2])
                    kn_toks[m] = ec.op("v", lambda v, m=m, bank=bank: v.scalar_tensor_tensor(
                        poolA[:, m * 512: m * 512 + TCW], bank[:, 0:TCW],
                        biasp[:, bcol + m:bcol + m + 1], varb[:, 0:TCW], OP.add, OP.mult),
                        deps=[t_r3, ("v", ec.cnt["v"])], psum=True)
                return kn_toks

            if lvl <= 1:
                return
            bank_first = {3: True, 4: True, 5: True, 6: True, 7: True}
            for tc in range(NTC):
                tdep = [("t", ec.cnt["t"])]
                h_toks = []
                for p in range(PT):
                    th, _ = emit_h_chunk(tc, p, ln_toks + tdep + (bar0 if tc == 0 else []))
                    h_toks.append(th)
                kn_toks = proj_norm_chunk(tc, "Wk", 48, h_toks, d_bk)
                if lvl <= 2:
                    continue
                vt_toks = [None] * (NKW * PT)
                for qt in range(4):
                    ev_toks = []
                    for m2 in range(2):
                        m = qt * 2 + m2
                        slot, wtok = ws.load(arena["Wv"] + m * 128, extra_deps=arena_tok["Wv"],
                                             colblock=(D, 128))
                        bank = pb[m2]
                        mmv = None
                        for k in range(PT):
                            mmv = ec.op("t", lambda te, k=k, bank=bank, slot=slot: te.matmul(
                                bank[:, 0:TCW], sw[:, slot * D + k * 128: slot * D + (k + 1) * 128],
                                poolB[:, k * 512: k * 512 + TCW], start=(k == 0), stop=(k == PT - 1)),
                                deps=[wtok, h_toks[k]] +
                                     ([("s", ec.cnt["s"]), ("v", ec.cnt["v"])] if k == 0 else []))
                        ws.done(slot, mmv)
                        ev_toks.append(ec.op("s", lambda s, m2=m2, m=m, bank=bank: s.activation(
                            stage[:, m2 * 512: m2 * 512 + TCW], bank[:, 0:TCW], AF.Identity,
                            bias=biasp[:, 56 + m:56 + m + 1], scale=1.0),
                            deps=[mmv, d_bv, ("t", ec.cnt["t"])], psum=True))
                    for m2 in range(2):
                        m = qt * 2 + m2
                        for sub in range(NKW):
                            mmt = ec.op("t", lambda te, m2=m2, sub=sub: te.transpose(
                                pb[2][:, (sub % 4) * 128:(sub % 4) * 128 + 128],
                                stage[:, m2 * 512 + sub * 128: m2 * 512 + sub * 128 + 128],
                                ident_f[:]),
                                deps=[ev_toks[m2], ("s", ec.cnt["s"]), ("v", ec.cnt["v"]), t_id])
                            vt_toks[sub * PT + m] = ec.op("s", lambda s, m=m, sub=sub: s.activation(
                                poolC[:, sub * 1024 + m * 128: sub * 1024 + (m + 1) * 128],
                                pb[2][:, (sub % 4) * 128:(sub % 4) * 128 + 128],
                                AF.Identity, bias=0.0, scale=1.0), deps=[mmt], psum=True)
                if lvl <= 3:
                    continue
                for sub in range(NKW):
                    vdep = [vt_toks[sub * PT + m] for m in range(PT)]
                    for h in range(NH):
                        po = (h % 2) * 64
                        mmkp = ec.op("t", lambda te, h=h, sub=sub, po=po: te.matmul(
                            pb[2][:, 0:NF],
                            poolA[po:po + 64, (h // 2) * 512 + sub * 128: (h // 2) * 512 + sub * 128 + 128],
                            projs[po:po + 64, (h // 2) * NF: (h // 2) * NF + NF],
                            start=True, stop=True),
                            deps=[kn_toks[h // 2], dproj, ("s", ec.cnt["s"]), ("v", ec.cnt["v"])])
                        t_kp = ec.op("s", lambda s: s.activation(
                            poolA[:, 4096: 4096 + NF], pb[2][:, 0:NF], AF.Relu, bias=0.0, scale=1.0),
                            deps=[mmkp, ("t", ec.cnt["t"])], psum=True)
                        for fc in range(2):
                            sidx = h * 2 + fc
                            bank = pb[4 + sidx // 8]
                            col = (sidx % 8) * 64
                            stf = bank_first[4 + sidx // 8] and tc == 0 and sub == 0
                            if stf:
                                bank_first[4 + sidx // 8] = False
                            fin = tc == NTC - 1 and sub == NKW - 1 and sidx % 8 == 7
                            ec.op("t", lambda te, h=h, sub=sub, fc=fc, bank=bank, col=col, st=stf,
                                  fin=fin: te.matmul(
                                bank[:, col:col + 64], poolA[:, 4096 + fc * 128: 4096 + (fc + 1) * 128],
                                poolC[:, sub * 1024 + h * 64: sub * 1024 + h * 64 + 64],
                                start=st, stop=fin, skip_group_check=not (st or fin)),
                                deps=[t_kp] + vdep +
                                     ([("v", ec.cnt["v"]), ("s", ec.cnt["s"])] if stf else []))
                            st2 = bank_first[3] and tc == 0 and sub == 0
                            if st2:
                                bank_first[3] = False
                            ec.op("t", lambda te, fc=fc, sidx=sidx, st=st2: te.matmul(
                                pb[3][:, sidx:sidx + 1], poolA[:, 4096 + fc * 128: 4096 + (fc + 1) * 128],
                                ones_b[:, 0:1], start=st, stop=False, skip_group_check=not st),
                                deps=[t_kp] + ([("v", ec.cnt["v"]), ("s", ec.cnt["s"])] if st2 else []))
                    for pr in range(PT):
                        fin3 = tc == NTC - 1 and sub == NKW - 1 and pr == PT - 1
                        ec.op("t", lambda te, pr=pr, sub=sub, fin3=fin3: te.matmul(
                            pb[3][:, 32 + pr:32 + pr + 1],
                            poolC[:, sub * 1024 + pr * 128: sub * 1024 + (pr + 1) * 128],
                            ones_b[:, 0:1], start=False, stop=fin3,
                            skip_group_check=not fin3), deps=vdep)
            if lvl <= 4:
                return
            bar1 = ec.bar()
            ev = []
            dkv_r = []
            for rnd in range(2):
                evr = []
                for b in range(2):
                    evr.append(ec.op("v", lambda v, b=b, rnd=rnd: v.tensor_copy(
                        stage[:, b * 512:(b + 1) * 512], pb[4 + rnd * 2 + b][:, 0:512]),
                        deps=bar1 + dkv_r, psum=True))
                dkv_r = [ec.dma("g", "cckv", lambda g, rnd=rnd: g.dma_start(
                    bass.AP(cc_in.tensor, rnd * 128 * 1024, [[1024, 128], [1, 1024]]),
                    stage[:, 0:1024]), deps=evr)]
            dkv1 = dkv_r[0]
            ev.append(ec.op("v", lambda v: v.tensor_copy(smallp[:, 104:136], pb[3][:, 0:32]),
                            deps=bar1, psum=True))
            ev.append(ec.op("v", lambda v: v.tensor_copy(smallp[:, 136:144], pb[3][:, 32:40]),
                            deps=bar1, psum=True))
            dkv2 = ec.dma("g", "cckv", lambda g: g.dma_start(
                bass.AP(cc_in.tensor, 128 * 2048, [[32, 128], [1, 32]]), smallp[:, 104:136]),
                deps=ev + [dkv1])
            mmtv = ec.op("t", lambda te: te.transpose(
                pb[2][0:8, 0:128], smallp[:, 136:144], ident_f[:]),
                deps=ev + [("s", ec.cnt["s"]), ("v", ec.cnt["v"])])
            t_tv = ec.op("v", lambda v: v.tensor_copy(stage[0:8, 0:128], pb[2][0:8, 0:128]),
                         deps=[mmtv, ("D:cckv", ec.dcnt["cckv"])], psum=True)
            dkv3 = ec.dma("g", "cckv", lambda g: g.dma_start(
                bass.AP(cc_in.tensor, 128 * 2048 + 32 * 128, [[128, 8], [1, 128]]),
                stage[0:8, 0:128]), deps=[t_tv])
            NCC = 128 * 2048 + 32 * 128 + 1024
            cckv = ec.cc(lambda g: g.collective_compute(
                "AllReduce", OP.add, replica_groups=groups,
                ins=[bass.AP(cc_in.tensor, 0, [[1, NCC]]).opt()],
                outs=[bass.AP(cc_out.tensor, 0, [[1, NCC]]).opt()]), deps=[dkv1, dkv2, dkv3])
            dr1a = ec.dma("g", "cckv", lambda g: g.dma_start(
                stage[:, 0:1024], bass.AP(cc_out.tensor, 0, [[1024, 128], [1, 1024]])),
                deps=[cckv, ("v", ec.cnt["v"])])
            t_kva = ec.op("v", lambda v: v.tensor_copy(poolC[:, 0:1024], stage[:, 0:1024]),
                          deps=[dr1a, ("t", ec.cnt["t"]), ("s", ec.cnt["s"])])
            dr1 = ec.dma("g", "cckv", lambda g: g.dma_start(
                stage[:, 0:1024], bass.AP(cc_out.tensor, 128 * 1024, [[1024, 128], [1, 1024]])),
                deps=[cckv, t_kva])
            dr2 = ec.dma("g", "cckv", lambda g: g.dma_start(
                smallp[:, 152:184], bass.AP(cc_out.tensor, 128 * 2048, [[32, 128], [1, 32]])),
                deps=[cckv, dr1])
            dr3 = ec.dma("g", "cckv", lambda g: g.dma_start(
                poolC[0:1, 2048:3072],
                bass.AP(cc_out.tensor, 128 * 2048 + 32 * 128, [[1024, 1], [1, 1024]])),
                deps=[cckv, dr2])
            t_kvb = ec.op("v", lambda v: v.tensor_copy(poolC[:, 1024:2048], stage[:, 0:1024]),
                          deps=[dr1, t_kva, ("t", ec.cnt["t"]), ("s", ec.cnt["s"])])
            t_vsb = dr3
            if lvl <= 5:
                return
            t_p1 = ec.op("g", lambda g: g.memset(poolC[0:1, 3072:3200], 0.1),
                         deps=[t_kvb, ("t", ec.cnt["t"]), ("s", ec.cnt["s"])])
            kvq0 = 4096
            bld = []
            prev_ev = {0: None, 1: None}
            for h in range(NH):
                for fc in range(2):
                    sidx = h * 2 + fc
                    bank = pb[sidx % 2]
                    mA = ec.op("t", lambda te, sidx=sidx, bank=bank: te.matmul(
                        bank[:, 0:64], ident_b[:], poolC[:, sidx * 64: sidx * 64 + 64],
                        start=True, stop=False),
                        deps=[t_kvb, t_idb, prev_ev[sidx % 2]] +
                             ([("s", ec.cnt["s"]), ("v", ec.cnt["v"])] if sidx < 2 else []))
                    mB = ec.op("t", lambda te, h=h, bank=bank: te.matmul(
                        bank[:, 0:64], poolC[0:1, 3072:3200], poolC[0:1, 2048 + h * 64: 2048 + h * 64 + 64],
                        start=False, stop=True), deps=[t_vsb, t_p1])
                    t_ev = ec.op("s", lambda s, sidx=sidx, bank=bank: s.activation(
                        poolA[:, kvq0 + sidx * 128: kvq0 + sidx * 128 + 64], bank[:, 0:64],
                        AF.Identity, bias=0.0, scale=1.0), deps=[mB, ("t", ec.cnt["t"])], psum=True)
                    prev_ev[sidx % 2] = t_ev
                    t_kd = ec.op("v", lambda v, sidx=sidx: v.tensor_scalar(
                        msq[:, 0:1], smallp[:, 152 + sidx:152 + sidx + 1], 0.1 * Tglob, None,
                        OP.add, OP.bypass), deps=[dr2, ("v", ec.cnt["v"])])
                    t_kdb = ec.op("v", lambda v, sidx=sidx: v.tensor_scalar(
                        poolA[:, kvq0 + sidx * 128 + 64: kvq0 + sidx * 128 + 128], ones_b[:, 0:64],
                        msq[0:128, 0:1], None, OP.mult, OP.bypass), deps=[t_kd])
                    bld += [t_ev, t_kdb]
            if int(os.environ.get("K_CORR", "9")) <= 1:
                return
            mmc = None
            for pr in range(PT):
                for which in range(2):
                    for part in range(2):
                        h = pr * 2 + part
                        po = part * 64
                        for fc in range(2):
                            sidx = h * 2 + fc
                            mmc = ec.op("t", lambda te, sidx=sidx, po=po, which=which, fc=fc,
                                        pr=pr: te.matmul(
                                pb[2][po:po + 64, 48 + pr * 2 + which: 48 + pr * 2 + which + 1],
                                poolA[:, kvq0 + sidx * 128 + which * 64: kvq0 + sidx * 128 + which * 64 + 64],
                                ones_b[:, 0:1], start=(fc == 0), stop=(fc == 1)),
                                deps=bld + ([("v", ec.cnt["v"]), ("s", ec.cnt["s"])]
                                            if pr == 0 and which == 0 and fc == 0 else []))
            t_corr = ec.op("v", lambda v: v.tensor_scalar(
                smallp[:, 200:216], pb[2][:, 48:64], 0.1, None,
                OP.mult, OP.bypass), deps=[mmc, ("v", ec.cnt["v"])], psum=True)
            corr = {(pr, which): t_corr for pr in range(PT) for which in range(2)}
            if lvl <= 6:
                return
            for tc in range(NTC):
                c0 = tc * TCW
                tdep = [("t", ec.cnt["t"])]
                h_toks = []
                for p in range(PT):
                    th, _ = emit_h_chunk(tc, p, ln_toks + tdep)
                    h_toks.append(th)
                qn_toks = proj_norm_chunk(tc, "Wq", 40, h_toks, d_bq)
                if lvl <= 7:
                    continue
                at_toks = []
                for pr in range(PT):
                    dv = {}
                    for part in range(2):
                        h = pr * 2 + part
                        po = part * 64
                        qp_ev = [None, None]
                        for fc in range(2):
                            mmqp = ec.op("t", lambda te, h=h, po=po, fc=fc: te.matmul(
                                pb[3 + fc][:, 0:TCW],
                                projs[po:po + 64, (h // 2) * NF + fc * 128: (h // 2) * NF + (fc + 1) * 128],
                                poolA[po:po + 64, (h // 2) * 512: (h // 2) * 512 + TCW],
                                start=True, stop=True),
                                deps=[qn_toks[h // 2], dproj, ("s", ec.cnt["s"])])
                            qp_ev[fc] = ec.op("s", lambda s, fc=fc: s.activation(
                                poolB[:, fc * 512: fc * 512 + TCW], pb[3 + fc][:, 0:TCW],
                                AF.Relu, bias=0.0, scale=1.0),
                                deps=[mmqp, ("t", ec.cnt["t"]), ("v", ec.cnt["v"])], psum=True)
                        for fc in range(2):
                            sidx = h * 2 + fc
                            stq = fc == 0
                            fq = fc == 1
                            ec.op("t", lambda te, sidx=sidx, po=po, fc=fc, stq=stq, fq=fq: te.matmul(
                                pb[5][po:po + 64, 0:TCW],
                                poolA[:, kvq0 + sidx * 128: kvq0 + sidx * 128 + 64],
                                poolB[:, fc * 512: fc * 512 + TCW], start=stq, stop=fq,
                                skip_group_check=not (stq or fq)),
                                deps=[qp_ev[fc], ("s", ec.cnt["s"]), ("v", ec.cnt["v"])] + bld)
                            ec.op("t", lambda te, sidx=sidx, po=po, fc=fc, stq=stq, fq=fq: te.matmul(
                                pb[6][po:po + 64, 0:TCW],
                                poolA[:, kvq0 + sidx * 128 + 64: kvq0 + sidx * 128 + 128],
                                poolB[:, fc * 512: fc * 512 + TCW], start=stq, stop=fq,
                                skip_group_check=not (stq or fq)),
                                deps=[qp_ev[fc]])
                    t_nm = ec.op("s", lambda s, pr=pr: s.activation(
                        dbuf[:, 0:TCW], pb[5][:, 0:TCW], AF.Identity,
                        bias=smallp[:, 200 + pr * 2: 200 + pr * 2 + 1], scale=1.0),
                        deps=[("t", ec.cnt["t"]), corr[(pr, 0)], ("v", ec.cnt["v"])], psum=True)
                    t_dn = ec.op("s", lambda s, pr=pr: s.activation(
                        hbuf[:, 0:TCW], pb[6][:, 0:TCW], AF.Ln,
                        bias=smallp[:, 200 + pr * 2 + 1: 200 + pr * 2 + 2], scale=1.0),
                        deps=[("t", ec.cnt["t"]), corr[(pr, 1)]], psum=True)
                    t_dn2 = ec.op("s", lambda s: s.activation(
                        hbuf[:, 0:TCW], hbuf[:, 0:TCW], AF.Exp, bias=0.0, scale=-1.0),
                        deps=[t_dn])
                    at_toks.append(ec.op("v", lambda v, pr=pr: v.tensor_tensor(
                        poolC[:, pr * 512: pr * 512 + TCW], dbuf[:, 0:TCW], hbuf[:, 0:TCW],
                        OP.mult), deps=[t_nm, t_dn2, ("v", ec.cnt["v"])]))
                if lvl <= 8:
                    continue
                for m in range(PT):
                    slot, wtok = ws.load(arena["Wo0"] + m * 128, extra_deps=arena_tok["Wo0"],
                                         colblock=(D, 128))
                    bank = pb[7] if m % 2 == 0 else pb[2]
                    mmo = None
                    for kt in range(PT):
                        mmo = ec.op("t", lambda te, kt=kt, bank=bank, slot=slot: te.matmul(
                            bank[:, 0:TCW], sw[:, slot * D + kt * 128: slot * D + (kt + 1) * 128],
                            poolC[:, kt * 512: kt * 512 + TCW], start=(kt == 0), stop=(kt == PT - 1)),
                            deps=[wtok, at_toks[kt]] +
                                 ([("s", ec.cnt["s"]), ("v", ec.cnt["v"])] if kt == 0 else []))
                    ws.done(slot, mmo)
                    t_u = ec.op("v", lambda v, m=m, bank=bank: v.tensor_scalar(
                        varb[:, 0:TCW], bank[:, 0:TCW], biasp[:, 64 + m:64 + m + 1], None,
                        OP.add, OP.bypass), deps=[mmo, d_bo0, ("v", ec.cnt["v"])], psum=True)
                    ec.op("v", lambda v, m=m, c0=c0: v.tensor_tensor(
                        X[:, m * Tc + c0: m * Tc + c0 + TCW],
                        X[:, m * Tc + c0: m * Tc + c0 + TCW], varb[:, 0:TCW], OP.add),
                        deps=[t_u])


        def emit_l1ffn(tag):
            T_loc = float(Tc)
            ln_toks = emit_ln(ec.bar())
            bar0 = ec.bar()
            d_bb = ec.dma("y", "mbb", lambda y: y.dma_start(biasp[:, 72:80], W["b_base"]), deps=bar0)
            sh = 84
            hw_toks = []
            hwl = {}
            for tc in range(NTC):
                for p in range(PT):
                    th, _ = emit_h_chunk(tc, p, ln_toks + [hwl.get(p)] +
                                         (bar0 if tc == 0 and p == 0 else []))
                    if tc == 0:
                        ec.op("v", lambda v, p=p: v.tensor_scalar(
                            msq[:, 0:TCW], poolB[:, p * 512: p * 512 + TCW], 1.0, 0.0,
                            OP.mult, OP.add, accum_out=smallp[:, sh + p: sh + p + 1]),
                            deps=[th, ("v", ec.cnt["v"])])
                    else:
                        ec.op("v", lambda v, p=p: v.tensor_scalar(
                            msq[:, 0:TCW], poolB[:, p * 512: p * 512 + TCW], 1.0,
                            smallp[0:128, sh + p: sh + p + 1],
                            OP.mult, OP.add, accum_out=smallp[:, sh + p: sh + p + 1]),
                            deps=[th, ("v", ec.cnt["v"])])
                    hw_toks.append(ec.dma("y", f"hs{p}", lambda y, tc=tc, p=p: y.dma_start(
                        bass.AP(h_scr.tensor, p * 128 * Tc + tc * TCW, [[Tc, 128], [1, TCW]]),
                        poolB[:, p * 512: p * 512 + TCW]), deps=[th]))
                    hwl[p] = hw_toks[-1]
            sumh_tok = ("v", ec.cnt["v"])
            acc_cols = {"a0": 102, "a1": 103, "b0": 104, "b1": 105}
            loss_toks = []
            for e in range(NE):
                ebar = [("t", ec.cnt["t"]), ("s", ec.cnt["s"]), ("v", ec.cnt["v"])]

                d_bd2 = ec.dma("y", "mbd2", lambda y, e=e: y.dma_start(
                    biasp[:, 80:88], bass.AP(W["bd2"].tensor, e * D, [[8, 128], [1, 8]])),
                    deps=ebar)
                gm_mm = {0: None, 1: None}
                mmw = None
                for hf in range(2):
                    d_w2f = ec.dma("y", "wd2f", lambda y, e=e, hf=hf: y.dma_start(
                        stage[:, 0:1024], bass.AP(W["Wd2T"].tensor, e * D * HD1 + hf * 4 * 128 * HD1,
                                          [[HD1, 128], [128 * HD1, 4], [1, HD1]])),
                        deps=hw_toks[-2:] + ebar + [("t", ec.cnt["t"])])
                    for m2 in range(2):
                        for kt4 in range(4):
                            gm_mm[m2] = ec.op("t", lambda te, m2=m2, kt4=kt4, hf=hf: te.matmul(
                                pb[m2][:, 0:256],
                                stage[:, kt4 * 256 + m2 * 128: kt4 * 256 + (m2 + 1) * 128],
                                stage[:, kt4 * 256: (kt4 + 1) * 256],
                                start=(kt4 == 0 and hf == 0), stop=(kt4 == 3 and hf == 1)),
                                deps=[d_w2f] + ([("v", ec.cnt["v"])] if kt4 == 0 and hf == 0 else []) +
                                     (ebar if m2 == 0 and kt4 == 0 and hf == 0 else []))
                    for kt4 in range(4):
                        mmw = ec.op("t", lambda te, kt4=kt4, hf=hf: te.matmul(
                            pb[2][0:1, 0:256], biasp[:, 80 + hf * 4 + kt4:80 + hf * 4 + kt4 + 1],
                            stage[:, kt4 * 256: (kt4 + 1) * 256],
                            start=(kt4 == 0 and hf == 0), stop=(kt4 == 3 and hf == 1)),
                            deps=[d_w2f, d_bd2])
                gm_toks = []
                for m2 in range(2):
                    gm_toks.append(ec.op("v", lambda v, m2=m2: v.tensor_copy(
                        gbuf[:, m2 * 256:(m2 + 1) * 256], pb[m2][:, 0:256]),
                        deps=[gm_mm[m2], ("v", ec.cnt["v"])], psum=True))
                t_w2b = ec.op("v", lambda v: v.tensor_scalar(
                    msq[0:1, 256:512], pb[2][0:1, 0:256], 2.0, None, OP.mult, OP.bypass),
                    deps=[mmw, ("v", ec.cnt["v"])], psum=True)
                t_bsq = ec.op("s", lambda s: s.activation(
                    msq[:, 0:8], biasp[:, 80:88], AF.Square, bias=0.0, scale=1.0),
                    deps=[d_bd2, ("v", ec.cnt["v"]), ("t", ec.cnt["t"])])
                t_bsc = ec.op("v", lambda v: v.tensor_reduce(
                    smallp[:, 100:101], msq[:, 0:8], axis=AX.X, op=OP.add), deps=[t_bsq])
                t_bsh0 = ec.op("v", lambda v: v.tensor_tensor(
                    msq[:, 0:8], biasp[:, 80:88], smallp[:, sh:sh + 8], OP.mult),
                    deps=[d_bd2, sumh_tok, ("v", ec.cnt["v"])])
                t_bsh = ec.op("v", lambda v: v.tensor_scalar(
                    varb[:, 0:8], msq[:, 0:8], 1.0, 0.0, OP.mult, OP.add,
                    accum_out=smallp[:, 101:102]), deps=[t_bsh0])
                last_acc = {}
                for tc in range(NTC):
                    hr_toks = []
                    for p in range(PT):
                        hr_toks.append(ec.dma("y", f"hr{p}", lambda y, tc=tc, p=p: y.dma_start(
                            poolB[:, p * 512: p * 512 + TCW],
                            bass.AP(h_scr.tensor, p * 128 * Tc + tc * TCW, [[Tc, 128], [1, TCW]])),
                            deps=hw_toks + [("t", ec.cnt["t"]), ("v", ec.cnt["v"])]))
                    hv_toks = []
                    for m2 in range(2):
                        slot, wtok = ws.load(arena["Wd1"] + e * D * HD1 + m2 * 128,
                                             extra_deps=arena_tok["Wd1"], colblock=(HD1, 128))
                        mmh = None
                        for kt in range(PT):
                            mmh = ec.op("t", lambda te, kt=kt, m2=m2, slot=slot: te.matmul(
                                pb[m2][:, 0:TCW], sw[:, slot * D + kt * 128: slot * D + (kt + 1) * 128],
                                poolB[:, kt * 512: kt * 512 + TCW], start=(kt == 0), stop=(kt == PT - 1)),
                                deps=[wtok, hr_toks[kt]] +
                                     ([("v", ec.cnt["v"]), ("s", ec.cnt["s"])] if kt == 0 else []))
                        ws.done(slot, mmh)
                        if tc == 0 and m2 == 0:
                            d_bd1 = ec.dma("y", "mbd1", lambda y, e=e: y.dma_start(
                                biasp[:, 88:90], bass.AP(W["bd1"].tensor, e * 256, [[2, 128], [1, 2]])),
                                deps=ebar)
                            emit_l1ffn.d_bd1 = d_bd1
                        hv_toks.append(ec.op("s", lambda s, m2=m2: s.activation(
                            stage[:, m2 * 512: m2 * 512 + TCW], pb[m2][:, 0:TCW], AF.Relu,
                            bias=biasp[:, 88 + m2:88 + m2 + 1], scale=1.0),
                            deps=[mmh, emit_l1ffn.d_bd1], psum=True))
                    z_toks = []
                    for m2 in range(2):
                        slot, wtok = ws.load(arena["Wd2T"] + e * D * HD1 + m2 * 128,
                                             extra_deps=arena_tok["Wd2T"], colblock=(HD1, 128))
                        mmz = None
                        for kt in range(PT):
                            mmz = ec.op("t", lambda te, kt=kt, m2=m2, slot=slot: te.matmul(
                                pb[2 + m2][:, 0:TCW], sw[:, slot * D + kt * 128: slot * D + (kt + 1) * 128],
                                poolB[:, kt * 512: kt * 512 + TCW], start=(kt == 0), stop=(kt == PT - 1)),
                                deps=[wtok, hr_toks[kt]] +
                                     ([("v", ec.cnt["v"])] if kt == 0 else []))
                        ws.done(slot, mmz)
                        z_toks.append(mmz)
                    q_toks = []
                    for m2 in range(2):
                        mmq = None
                        for k2 in range(2):
                            mmq = ec.op("t", lambda te, m2=m2, k2=k2: te.matmul(
                                pb[4 + m2][:, 0:TCW], gbuf[:, k2 * 256 + m2 * 128: k2 * 256 + (m2 + 1) * 128],
                                stage[:, k2 * 512: k2 * 512 + TCW], start=(k2 == 0), stop=False),
                                deps=[gm_toks[k2], hv_toks[k2]] +
                                     ([("v", ec.cnt["v"])] if k2 == 0 else []))
                        for cq in range(TCW // 128):
                            mmq = ec.op("t", lambda te, m2=m2, cq=cq: te.matmul(
                                pb[4 + m2][:, cq * 128:(cq + 1) * 128],
                                msq[0:1, 256 + m2 * 128:256 + (m2 + 1) * 128],
                                ones_f[0:1, 0:128], start=False, stop=(cq == TCW // 128 - 1),
                                skip_group_check=(cq != TCW // 128 - 1)), deps=[t_w2b])
                        q_toks.append(mmq)
                    for m2 in range(2):
                        for nm, bank, tok in (("a", pb[4 + m2], q_toks[m2]), ("b", pb[2 + m2], z_toks[m2])):
                            key = nm + str(m2)
                            col = acc_cols[key]
                            seed = 0.0 if key not in last_acc else smallp[0:128, col + 8:col + 9] \
                                if False else (0.0 if key not in last_acc else smallp[0:128, col:col + 1])
                            dst = smallp[:, col:col + 1]
                            t_pr = ec.op("v", lambda v, m2=m2, bank=bank: v.tensor_tensor(
                                varb[:, 0:TCW], stage[:, m2 * 512: m2 * 512 + TCW], bank[:, 0:TCW],
                                OP.mult), deps=[tok, hv_toks[m2], ("v", ec.cnt["v"])], psum=True)
                            if key not in last_acc:
                                t_acc = ec.op("v", lambda v, dst=dst: v.tensor_scalar(
                                    dbuf[:, 0:TCW], varb[:, 0:TCW], 1.0, 0.0, OP.mult, OP.add,
                                    accum_out=dst), deps=[t_pr])
                            else:
                                t_acc = ec.op("v", lambda v, dst=dst, col=col: v.tensor_scalar(
                                    dbuf[:, 0:TCW], varb[:, 0:TCW], 1.0, smallp[0:128, col:col + 1],
                                    OP.mult, OP.add, accum_out=dst),
                                    deps=[t_pr, last_acc[key]])
                            last_acc[key] = t_acc
                t_c1 = ec.op("v", lambda v: v.tensor_tensor(
                    varb[:, 0:1], smallp[:, 102:103], smallp[:, 103:104], OP.add),
                    deps=[last_acc["a0"], last_acc["a1"], ("v", ec.cnt["v"])])
                t_c2 = ec.op("v", lambda v: v.tensor_tensor(
                    varb[:, 1:2], smallp[:, 104:105], smallp[:, 105:106], OP.add),
                    deps=[last_acc["b0"], last_acc["b1"], t_c1])
                t_c3 = ec.op("v", lambda v: v.scalar_tensor_tensor(
                    varb[:, 0:1], varb[:, 1:2], -2.0, varb[:, 0:1], OP.mult, OP.add), deps=[t_c2])
                t_c4 = ec.op("v", lambda v: v.scalar_tensor_tensor(
                    varb[:, 0:1], smallp[:, 100:101], T_loc, varb[:, 0:1], OP.mult, OP.add),
                    deps=[t_c3, t_bsc])
                t_c5 = ec.op("v", lambda v: v.scalar_tensor_tensor(
                    varb[:, 0:1], smallp[:, 101:102], -2.0, varb[:, 0:1], OP.mult, OP.add),
                    deps=[t_c4, t_bsh])
                mml = ec.op("t", lambda te: te.matmul(
                    pb[6][0:1, 0:1], varb[:, 0:1], ones_f[:, 0:1], start=True, stop=True),
                    deps=[t_c5])
                loss_toks.append(ec.op("v", lambda v, e=e: v.tensor_copy(
                    msq[0:1, 232 + e:233 + e], pb[6][0:1, 0:1]), deps=[mml, ("v", ec.cnt["v"])], psum=True))
            dl1 = ec.dma("g", "ccio", lambda g: g.dma_start(
                bass.AP(agr_in.tensor, 0, [[8, 1], [1, 8]]), msq[0:1, 232:240]), deps=loss_toks)
            ccl = ec.cc(lambda g: g.collective_compute(
                "AllReduce", OP.add, replica_groups=groups,
                ins=[bass.AP(agr_in.tensor, 0, [[1, 8]]).opt()],
                outs=[bass.AP(ag_out.tensor, 0, [[1, 8]]).opt()]), deps=[dl1])
            dl2 = ec.dma("g", "ccio", lambda g: g.dma_start(
                msq[0:1, 240:248], bass.AP(ag_out.tensor, 0, [[8, 1], [1, 8]])), deps=[ccl])
            t_mn = ec.op("v", lambda v: v.tensor_reduce(
                msq[0:1, 208:209], msq[0:1, 240:248], axis=AX.X, op=OP.min), deps=[dl2, ("v", ec.cnt["v"])])
            t_oh = ec.op("v", lambda v: v.tensor_scalar(
                msq[0:1, 216:224], msq[0:1, 240:248], msq[0:1, 208:209], None, OP.is_le, OP.bypass),
                deps=[t_mn])
            t_ix0 = ec.op("v", lambda v: v.tensor_tensor(
                msq[0:1, 216:224], msq[0:1, 216:224], iot8[0:1, 0:8], OP.mult),
                deps=[t_oh, d_ci])
            t_ix = ec.op("v", lambda v: v.tensor_scalar(
                varb[0:1, 0:NE], msq[0:1, 216:224], 1.0, 0.0, OP.mult, OP.add,
                accum_out=msq[0:1, 224:225]), deps=[t_ix0])
            t_ii = ec.op("v", lambda v: v.tensor_copy(idxi[:], msq[0:1, 224:225]), deps=[t_ix])
            ec.raw("g", lambda g: g.reg_load(selreg, idxi[0:1, 0:1]), deps=[t_ii])
            gkey = f"ga{tag}"
            gdeps2 = ec._waits("g", ec.bar())
            if gkey not in ec.dsem:
                ec.dsem[gkey] = es.enter_context(nc.semaphore("dk_" + gkey))
                ec.dcnt[gkey] = 0
            ec.dcnt[gkey] += 64

            def _gad(g, gdeps2=gdeps2):
                for s_, v_ in gdeps2:
                    g.wait_ge(s_, v_)
                for e in range(NE):
                    with g.If_eq(selreg, e):
                        g.dma_start(poolA[:, 1536:1536 + 8 * RAD],
                                    bass.AP(W["Wa_down"].tensor, e * D * RAD,
                                            [[RAD, 128], [128 * RAD, PT], [1, RAD]])
                                    ).then_inc(ec.dsem[gkey], 16)
                        g.dma_start(poolA[0:64, 512:512 + D],
                                    bass.AP(W["Wa_up"].tensor, e * RAD * D, [[D, 64], [1, D]])
                                    ).then_inc(ec.dsem[gkey], 16)
                        g.dma_start(smallp[0:64, 110:111],
                                    bass.AP(W["ba_down"].tensor, e * RAD, [[1, 64], [1, 1]])
                                    ).then_inc(ec.dsem[gkey], 16)
                        g.dma_start(biasp[:, 96:104],
                                    bass.AP(W["ba_up"].tensor, e * D, [[8, 128], [1, 8]])
                                    ).then_inc(ec.dsem[gkey], 16)
            ec.plan["g"].append(_gad)
            gad = ("D:" + gkey, ec.dcnt[gkey])
            t_cb = ec.op("v", lambda v: v.tensor_tensor(
                biasp[:, 88:96], biasp[:, 72:80], biasp[:, 96:104], OP.add),
                deps=[gad, d_bb, ("v", ec.cnt["v"]), ("s", ec.cnt["s"])])
            for tc in range(NTC):
                c0 = tc * TCW
                hr_toks = []
                for p in range(PT):
                    hr_toks.append(ec.dma("y", f"hr{p}", lambda y, tc=tc, p=p: y.dma_start(
                        poolB[:, p * 512: p * 512 + TCW],
                        bass.AP(h_scr.tensor, p * 128 * Tc + tc * TCW, [[Tc, 128], [1, TCW]])),
                        deps=[("t", ec.cnt["t"]), ("v", ec.cnt["v"])]))
                mma = None
                for kt in range(PT):
                    mma = ec.op("t", lambda te, kt=kt: te.matmul(
                        pb[6][0:64, 0:TCW], poolA[:, 1536 + kt * RAD:1536 + (kt + 1) * RAD],
                        poolB[:, kt * 512: kt * 512 + TCW], start=(kt == 0), stop=(kt == PT - 1)),
                        deps=[gad, hr_toks[kt]] + ([("v", ec.cnt["v"]), ("s", ec.cnt["s"])] if kt == 0 else []))
                t_a = ec.op("s", lambda s: s.activation(
                    poolA[0:64, 0:TCW], pb[6][0:64, 0:TCW], AF.Relu,
                    bias=smallp[0:64, 110:111], scale=1.0), deps=[mma, gad], psum=True)
                for m in range(PT):
                    slot, wtok = ws.load(arena["W_base"] + m * 128, extra_deps=arena_tok["W_base"],
                                         colblock=(D, 128))
                    bank = pb[7] if m % 2 == 0 else pb[2]
                    mmb = None
                    for kt in range(PT):
                        mmb = ec.op("t", lambda te, kt=kt, bank=bank, slot=slot: te.matmul(
                            bank[:, 0:TCW], sw[:, slot * D + kt * 128: slot * D + (kt + 1) * 128],
                            poolB[:, kt * 512: kt * 512 + TCW], start=(kt == 0), stop=False),
                            deps=[wtok, hr_toks[kt]] +
                                 ([("s", ec.cnt["s"]), ("v", ec.cnt["v"])] if kt == 0 else []))
                    mmb = ec.op("t", lambda te, m=m, bank=bank: te.matmul(
                        bank[:, 0:TCW], poolA[0:64, 512 + m * 128:512 + (m + 1) * 128],
                        poolA[0:64, 0:TCW], start=False, stop=True), deps=[t_a, gad])
                    ws.done(slot, mmb)
                    t_u = ec.op("v", lambda v, m=m, bank=bank: v.tensor_scalar(
                        varb[:, 0:TCW], bank[:, 0:TCW], biasp[:, 88 + m:88 + m + 1], 0.5,
                        OP.add, OP.mult), deps=[mmb, t_cb, ("v", ec.cnt["v"])], psum=True)
                    ec.op("v", lambda v, m=m, c0=c0: v.tensor_tensor(
                        X[:, m * Tc + c0: m * Tc + c0 + TCW],
                        X[:, m * Tc + c0: m * Tc + c0 + TCW], varb[:, 0:TCW], OP.add),
                        deps=[t_u])


        def emit_l1attn():
            def emit_rho_chunk(c0, ncols, deps):
                t1 = ec.op("s", lambda s: s.activation(
                    msq[:, 0:ncols], rstd_t[:, c0:c0 + ncols], AF.Square, bias=0.0, scale=1.0),
                    deps=list(deps) + [("s", ec.cnt["s"]), ("v", ec.cnt["v"])])
                t2 = ec.op("v", lambda v: v.tensor_scalar(
                    msq[:, 0:ncols], msq[:, 0:ncols], -1e-5, 1.0 + 1e-5, OP.mult, OP.add),
                    deps=[t1])
                t3 = ec.op("s", lambda s: s.activation(
                    msq[:, 0:ncols], msq[:, 0:ncols], AF.Ln, bias=0.0, scale=1.0), deps=[t2])
                t4 = ec.op("s", lambda s: s.activation(
                    msq[:, 0:ncols], msq[:, 0:ncols], AF.Exp, bias=0.0, scale=-0.5), deps=[t3])
                return ec.op("v", lambda v, c0=c0: v.tensor_tensor(
                    varb[:, 0:ncols], msq[:, 0:ncols], rstd_t[:, c0:c0 + ncols], OP.mult),
                    deps=[t4, ("v", ec.cnt["v"])])

            ln_toks = emit_ln(ec.bar())
            bar0 = ec.bar()
            YS = D * Tc
            d_lnw = ec.dma("y", "mlnw", lambda y: y.dma_start(biasp[:, 40:48], W["ln_w"]), deps=bar0)
            d_lnb = ec.dma("y", "mlnb", lambda y: y.dma_start(biasp[:, 48:56], W["ln_b"]), deps=bar0)
            d_dv = ec.dma("y", "mdv", lambda y: y.dma_start(biasp[:, 56:64], W["D_vec"]), deps=bar0)
            d_ob = ec.dma("y", "mob", lambda y: y.dma_start(biasp[:, 64:72], W["out_b"]), deps=bar0)
            d_l2w = ec.dma("y", "ml2w", lambda y: y.dma_start(biasp[:, 72:80], W["ln2_w"]), deps=bar0)
            d_l2b = ec.dma("y", "ml2b", lambda y: y.dma_start(biasp[:, 80:88], W["ln2_b"]), deps=bar0)
            d_f2b = ec.dma("y", "mf2b", lambda y: y.dma_start(biasp[:, 88:96], W["ffn2_b"]), deps=bar0)
            d_f1b = ec.dma("y", "mf1b", lambda y: y.dma_start(
                smallp[:, 112:144], W["ffn1_b"]), deps=bar0)
            d_al = ec.dma("y", "mal", lambda y: y.dma_start(
                stage[:, 0:128], bass.AP(W["A_log"].tensor, 0, [[16, 128], [2048, PT], [1, 16]])),
                deps=bar0)
            d_bm = ec.dma("y", "mbm", lambda y: y.dma_start(
                stage[:, 128:256], bass.AP(W["B_mat"].tensor, 0, [[16, 128], [2048, PT], [1, 16]])),
                deps=bar0)
            d_cm = ec.dma("y", "mcm", lambda y: y.dma_start(
                stage[:, 256:384], bass.AP(W["C_mat"].tensor, 0, [[16, 128], [2048, PT], [1, 16]])),
                deps=bar0)
            t_A = ec.op("s", lambda s: s.activation(stage[:, 0:128], stage[:, 0:128], AF.Sigmoid,
                        bias=0.0, scale=1.0), deps=[d_al, ("v", ec.cnt["v"]), ("t", ec.cnt["t"])])
            t_P = ec.op("v", lambda v: v.tensor_tensor(
                stage[:, 128:256], stage[:, 128:256], stage[:, 256:384], OP.mult),
                deps=[d_bm, d_cm, ("v", ec.cnt["v"])])
            kt_toks = []
            prev = t_P
            for dl in range(K_TAPS):
                for p in range(PT):
                    kt_toks.append(ec.op("v", lambda v, dl=dl, p=p: v.tensor_reduce(
                        gbuf[:, p * K_TAPS + dl: p * K_TAPS + dl + 1],
                        stage[:, 128 + p * 16:128 + (p + 1) * 16], axis=AX.X, op=OP.add),
                        deps=[prev]))
                if dl < K_TAPS - 1:
                    prev = ec.op("v", lambda v: v.tensor_tensor(
                        stage[:, 128:256], stage[:, 128:256], stage[:, 0:128], OP.mult),
                        deps=[t_A] + kt_toks[-PT:])
            kt_done = ("v", ec.cnt["v"])
            tl_toks = []
            c0t = Tc - 32
            t_rhoT = emit_rho_chunk(c0t, 32, bar0)
            t_rc = ec.op("v", lambda v: v.tensor_copy(hbuf[:, 64:96], varb[:, 0:32]), deps=[t_rhoT])
            for p in range(PT):
                t_d = ec.op("v", lambda v, p=p: v.tensor_tensor(
                    dbuf[:, 0:32], X[:, p * Tc + c0t: p * Tc + Tc], mean_t[:, c0t:Tc], OP.subtract),
                    deps=bar0 + [("v", ec.cnt["v"])])
                t_m = ec.op("v", lambda v, p=p: v.tensor_tensor(
                    hbuf[:, 0:32], dbuf[:, 0:32], hbuf[:, 64:96], OP.mult), deps=[t_d, t_rc])
                tl_toks.append(ec.op("v", lambda v, p=p: v.tensor_scalar(
                    poolB[:, 3840 + p * 32:3840 + (p + 1) * 32], hbuf[:, 0:32], biasp[0:128, 40 + p:41 + p],
                    biasp[0:128, 48 + p:49 + p], OP.mult, OP.add), deps=[t_m, d_lnw, d_lnb]))
            d_h1 = ec.dma("g", "hal", lambda g: g.dma_start(
                bass.AP(hal_in.tensor, 0, [[PT * 32, 128], [1, PT * 32]]), poolB[:, 3840:3840 + 256]), deps=tl_toks)
            ccha = ec.cc(lambda g: g.collective_compute(
                "AllGather", OP.bypass, replica_groups=groups,
                ins=[bass.AP(hal_in.tensor, 0, [[1, D * 32]]).opt()],
                outs=[bass.AP(hal_out.tensor, 0, [[1, gsz * D * 32]]).opt()]), deps=[d_h1])
            hs_toks = []
            for k in range(gsz):
                hs_toks.append(ec.dma("g", "hal", lambda g, k=k: g.dma_start(
                    stage[:, k * 256:(k + 1) * 256],
                    bass.AP(hal_out.tensor, k * D * 32, [[PT * 32, 128], [1, PT * 32]])),
                    deps=[ccha, ("s", ec.cnt["s"]), ("t", ec.cnt["t"]), ("v", ec.cnt["v"])]))
            t_hc = ec.op("v", lambda v: v.tensor_scalar(
                poolB[:, 3840:3840 + 256], stage[:, 0:256], biasp[0:128, 104:105], None, OP.mult, OP.bypass),
                deps=hs_toks + [d_cm, ("v", ec.cnt["v"])])
            for k in range(1, gsz):
                t_hc = ec.op("v", lambda v, k=k: v.scalar_tensor_tensor(
                    poolB[:, 3840:3840 + 256], stage[:, k * 256:(k + 1) * 256], biasp[0:128, 104 + k:105 + k],
                    poolB[:, 3840:3840 + 256], OP.mult, OP.add), deps=[t_hc, hs_toks[k]])
            XN0 = 32
            ys_last = {0: None, 1: None}
            for p in range(PT):
                pdep = [("t", ec.cnt["t"]), ("s", ec.cnt["s"])]
                t_hcp = ec.op("v", lambda v, p=p: v.tensor_copy(
                    poolA[:, 0:32], poolB[:, 3840 + p * 32:3840 + (p + 1) * 32]), deps=[t_hc] + pdep +
                    [("v", ec.cnt["v"])])
                xn_toks = []
                for tc in range(NTC):
                    c0 = tc * TCW
                    t_rho = emit_rho_chunk(c0, TCW, ln_toks + pdep)
                    t_d = ec.op("v", lambda v, p=p, c0=c0: v.tensor_tensor(
                        dbuf[:, 0:TCW], X[:, p * Tc + c0: p * Tc + c0 + TCW],
                        mean_t[:, c0:c0 + TCW], OP.subtract), deps=ln_toks + pdep + [("v", ec.cnt["v"])])
                    t_m = ec.op("v", lambda v, c0=c0: v.tensor_tensor(
                        hbuf[:, 0:TCW], dbuf[:, 0:TCW], varb[:, 0:TCW], OP.mult), deps=[t_d, t_rho])
                    xn_toks.append(ec.op("v", lambda v, p=p, c0=c0: v.tensor_scalar(
                        poolA[:, XN0 + c0: XN0 + c0 + TCW], hbuf[:, 0:TCW],
                        biasp[0:128, 40 + p:41 + p], biasp[0:128, 48 + p:49 + p],
                        OP.mult, OP.add), deps=[t_m, d_lnw, d_lnb]))
                dg_toks = []
                for dl in range(K_TAPS):
                    dg_toks.append(ec.op("v", lambda v, dl=dl, p=p: v.tensor_scalar(
                        poolC[:, dl * 128:(dl + 1) * 128], ident_b[:],
                        gbuf[0:128, p * K_TAPS + dl: p * K_TAPS + dl + 1], None,
                        OP.mult, OP.bypass), deps=[kt_done, t_idb] + pdep +
                        ([("v", ec.cnt["v"])] if dl == 0 else [])))
                for tc in range(NTC):
                    c0 = tc * TCW
                    bank = pb[tc % 2]
                    mmc = None
                    for dl in range(K_TAPS):
                        mmc = ec.op("t", lambda te, dl=dl, c0=c0, bank=bank: te.matmul(
                            bank[:, 0:TCW], poolC[:, dl * 128:(dl + 1) * 128],
                            poolA[:, XN0 + c0 - dl: XN0 + c0 - dl + TCW],
                            start=(dl == 0), stop=(dl == K_TAPS - 1)),
                            deps=[dg_toks[dl], xn_toks[tc]] + ([t_hcp] if tc == 0 else []) +
                                 ([("v", ec.cnt["v"]), ("s", ec.cnt["s"])] if dl == 0 else []))
                    t_y = ec.op("v", lambda v, p=p, c0=c0, bank=bank: v.scalar_tensor_tensor(
                        varb[:, 0:TCW], poolA[:, XN0 + c0: XN0 + c0 + TCW],
                        biasp[0:128, 56 + p:57 + p], bank[:, 0:TCW], OP.mult, OP.add),
                        deps=[mmc, d_dv, ("v", ec.cnt["v"])], psum=True)
                    t_sl = ec.op("s", lambda s: s.activation(
                        msq[:, 0:TCW], varb[:, 0:TCW], AF.Silu, bias=0.0, scale=1.0),
                        deps=[t_y, ("s", ec.cnt["s"])])
                    slotc = tc % 2
                    t_yb = ec.op("v", lambda v, slotc=slotc: v.tensor_copy(
                        poolB[:, slotc * 512: slotc * 512 + TCW], msq[:, 0:TCW]),
                        deps=[t_sl, ys_last[slotc], ("v", ec.cnt["v"])])
                    ys_last[slotc] = ec.dma("y", f"ys{slotc}", lambda y, p=p, c0=c0, slotc=slotc:
                                            y.dma_start(
                        bass.AP(h_scr.tensor, YS + p * 128 * Tc + c0, [[Tc, 128], [1, TCW]]),
                        poolB[:, slotc * 512: slotc * 512 + TCW]), deps=[t_yb])
            stat_first = [True]
            x1_last = {}
            for tc in range(NTC):
                c0 = tc * TCW
                tdep = [("t", ec.cnt["t"]), ("s", ec.cnt["s"])]
                yr_toks = []
                for p in range(PT):
                    yr_toks.append(ec.dma("y", f"yr{p}", lambda y, p=p, c0=c0: y.dma_start(
                        poolB[:, p * 512: p * 512 + TCW],
                        bass.AP(h_scr.tensor, YS + p * 128 * Tc + c0, [[Tc, 128], [1, TCW]])),
                        deps=[ys_last[0], ys_last[1], ("t", ec.cnt["t"]), ("v", ec.cnt["v"])]))
                h_toks = []
                for p in range(PT):
                    th, _ = emit_h_chunk(tc, p, ln_toks + tdep, add_to_x=1.0,
                                         dst=poolA, dst_col=p * 512)
                    h_toks.append(th)
                for m in range(PT):
                    slot, wtok = ws.load(arena["out_wT"] + m * 128, extra_deps=arena_tok["out_wT"],
                                         colblock=(D, 128))
                    bank = pb[2 + m % 2]
                    mmo = None
                    for kt in range(PT):
                        mmo = ec.op("t", lambda te, kt=kt, bank=bank, slot=slot: te.matmul(
                            bank[:, 0:TCW], sw[:, slot * D + kt * 128: slot * D + (kt + 1) * 128],
                            poolB[:, kt * 512: kt * 512 + TCW], start=(kt == 0), stop=(kt == PT - 1)),
                            deps=[wtok, yr_toks[kt]] +
                                 ([("s", ec.cnt["s"]), ("v", ec.cnt["v"])] if kt == 0 else []))
                    ws.done(slot, mmo)
                    t_yp = ec.op("v", lambda v, m=m, bank=bank: v.tensor_scalar(
                        varb[:, 0:TCW], bank[:, 0:TCW], biasp[:, 64 + m:64 + m + 1], None,
                        OP.add, OP.bypass), deps=[mmo, d_ob, ("v", ec.cnt["v"])], psum=True)
                    ec.op("v", lambda v, m=m, c0=c0: v.tensor_tensor(
                        X[:, m * Tc + c0: m * Tc + c0 + TCW],
                        X[:, m * Tc + c0: m * Tc + c0 + TCW], varb[:, 0:TCW], OP.add),
                        deps=[t_yp])
                    t_x1 = ec.op("v", lambda v, m=m: v.tensor_tensor(
                        poolC[:, m * 512: m * 512 + TCW], varb[:, 0:TCW],
                        poolA[:, m * 512: m * 512 + TCW], OP.add),
                        deps=[t_yp, h_toks[m], x1_last.get(m)])
                    x1_last[m] = ec.dma("y", f"x1{m}", lambda y, m=m, c0=c0: y.dma_start(
                        bass.AP(h_scr.tensor, m * 128 * Tc + c0, [[Tc, 128], [1, TCW]]),
                        poolC[:, m * 512: m * 512 + TCW]), deps=[t_x1])
                    mmsa = ec.op("t", lambda te, m=m: te.matmul(
                        pb[4][:, 0:TCW], ones_b[:, 0:128], poolC[:, m * 512: m * 512 + TCW],
                        start=(m == 0), stop=(m == PT - 1)),
                        deps=[t_x1] + ([("v", ec.cnt["v"]), ("s", ec.cnt["s"])] if m == 0 else []))
                    t_sq = ec.op("s", lambda s, m=m: s.activation(
                        poolA[:, 4096 + 512: 4096 + 512 + TCW], poolC[:, m * 512: m * 512 + TCW],
                        AF.Square, bias=0.0, scale=1.0),
                        deps=[t_x1, ("s", ec.cnt["s"]), ("t", ec.cnt["t"])])
                    mmsb = ec.op("t", lambda te, m=m: te.matmul(
                        pb[5][:, 0:TCW], ones_b[:, 0:128], poolA[:, 4096 + 512: 4096 + 512 + TCW],
                        start=(m == 0), stop=(m == PT - 1)), deps=[t_sq])
                t_mean2 = ec.op("v", lambda v, c0=c0: v.tensor_scalar(
                    mean_t[:, c0:c0 + TCW], pb[4][:, 0:TCW], 1.0 / D, None, OP.mult, OP.bypass),
                    deps=[mmsa, ("v", ec.cnt["v"])], psum=True)
                t_msq2 = ec.op("s", lambda s: s.activation(
                    msq[:, 0:TCW], pb[4][:, 0:TCW], AF.Square, bias=0.0, scale=1.0 / D),
                    deps=[mmsa, ("s", ec.cnt["s"])], psum=True)
                t_var2 = ec.op("v", lambda v: v.scalar_tensor_tensor(
                    varb[:, 0:TCW], pb[5][:, 0:TCW], 1.0 / D, msq[:, 0:TCW], OP.mult, OP.subtract),
                    deps=[mmsb, t_msq2, t_mean2], psum=True)
                t_ln2 = ec.op("s", lambda s: s.activation(
                    msq[:, 0:TCW], varb[:, 0:TCW], AF.Ln, bias=eps_t[:], scale=1.0), deps=[t_var2])
                ec.op("s", lambda s, c0=c0: s.activation(
                    rstd_t[:, c0:c0 + TCW], msq[:, 0:TCW], AF.Exp, bias=0.0, scale=-0.5),
                    deps=[t_ln2])
            bar2 = ec.bar()
            for tc in range(NTC):
                c0 = tc * TCW
                x1r = []
                for p in range(PT):
                    x1r.append(ec.dma("y", f"xr{p}", lambda y, p=p, c0=c0: y.dma_start(
                        poolB[:, p * 512: p * 512 + TCW],
                        bass.AP(h_scr.tensor, p * 128 * Tc + c0, [[Tc, 128], [1, TCW]])),
                        deps=[x1_last[p], ("t", ec.cnt["t"]), ("v", ec.cnt["v"])]))
                xn2_toks = []
                for p in range(PT):
                    t_d = ec.op("v", lambda v, p=p, c0=c0: v.tensor_tensor(
                        dbuf[:, 0:TCW], poolB[:, p * 512: p * 512 + TCW], mean_t[:, c0:c0 + TCW],
                        OP.subtract), deps=[x1r[p], ("v", ec.cnt["v"])] + bar2)
                    t_m = ec.op("v", lambda v, c0=c0: v.tensor_tensor(
                        hbuf[:, 0:TCW], dbuf[:, 0:TCW], rstd_t[:, c0:c0 + TCW], OP.mult), deps=[t_d])
                    xn2_toks.append(ec.op("v", lambda v, p=p: v.tensor_scalar(
                        poolA[:, p * 512: p * 512 + TCW], hbuf[:, 0:TCW],
                        biasp[0:128, 72 + p:73 + p], biasp[0:128, 80 + p:81 + p],
                        OP.mult, OP.add), deps=[t_m, d_l2w, d_l2b]))
                for ts_ in range(NKW):
                    tcol = ts_ * 128
                    hdn_toks = []
                    for mh in range(H1 // 128):
                        slot, wtok = ws.load(arena["ffn1_wT"] + mh * 128,
                                             extra_deps=arena_tok["ffn1_wT"], colblock=(H1, 128))
                        bank = pb[mh % 2]
                        mmh = None
                        for kt in range(PT):
                            mmh = ec.op("t", lambda te, kt=kt, bank=bank, slot=slot, tcol=tcol: te.matmul(
                                bank[:, 0:128], sw[:, slot * D + kt * 128: slot * D + (kt + 1) * 128],
                                poolA[:, kt * 512 + tcol: kt * 512 + tcol + 128],
                                start=(kt == 0), stop=(kt == PT - 1)),
                                deps=[wtok, xn2_toks[kt]] +
                                     ([("s", ec.cnt["s"]), ("v", ec.cnt["v"])] if kt == 0 else []))
                        ws.done(slot, mmh)
                        hdn_toks.append(ec.op("s", lambda s, mh=mh, bank=bank: s.activation(
                            poolC[:, mh * 128:(mh + 1) * 128], bank[:, 0:128], AF.Gelu,
                            bias=smallp[:, 112 + mh:112 + mh + 1], scale=1.0),
                            deps=[mmh, d_f1b], psum=True))
                    mm2_last = [None] * PT
                    for kh in range(H1 // 128):
                        slot, wtok = ws.load(arena["ffn2_wT"] + kh * 128 * D,
                                             extra_deps=arena_tok["ffn2_wT"])
                        for mo in range(PT):
                            mm2_last[mo] = ec.op("t", lambda te, kh=kh, mo=mo, slot=slot: te.matmul(
                                pb[mo][:, 128:256], sw[:, slot * D + mo * 128: slot * D + (mo + 1) * 128],
                                poolC[:, kh * 128:(kh + 1) * 128],
                                start=(kh == 0), stop=(kh == H1 // 128 - 1)),
                                deps=[wtok, hdn_toks[kh]] +
                                     ([("s", ec.cnt["s"]), ("v", ec.cnt["v"])] if kh == 0 and mo == 0 else []))
                        ws.done(slot, mm2_last[PT - 1])
                    for mo in range(PT):
                        t_u = ec.op("v", lambda v, mo=mo: v.tensor_scalar(
                            varb[:, 0:128], pb[mo][:, 128:256], biasp[:, 88 + mo:88 + mo + 1], None,
                            OP.add, OP.bypass), deps=[mm2_last[mo], d_f2b, ("v", ec.cnt["v"])], psum=True)
                        ec.op("v", lambda v, mo=mo, c0=c0, tcol=tcol: v.tensor_tensor(
                            X[:, mo * Tc + c0 + tcol: mo * Tc + c0 + tcol + 128],
                            X[:, mo * Tc + c0 + tcol: mo * Tc + c0 + tcol + 128],
                            varb[:, 0:128], OP.add), deps=[t_u])
        if 1 in enable:
            emit_l0ffn(1)
        if 2 in enable:
            emit_l0attn()
        if 3 in enable:
            emit_l0ffn(3)
        if 4 in enable:
            emit_l1ffn(4)
        if 5 in enable:
            emit_l1attn()
        if 6 in enable:
            emit_l1ffn(6)

        barf = ec.bar()
        st_toks = []
        bank_rd2 = {}
        stage_st = None
        for t in range(NT128 if not os.environ.get("K_NOOUT") else 0):
            stg = stage[:, 0:D]
            cp = []
            for p in range(PT):
                bi = 4 + p % 4
                bank = pb[bi]
                ttok = ec.op("t", lambda te, p=p, t=t, bank=bank: te.transpose(
                    bank[:, 0:128], X[:, p * Tc + t * 128: p * Tc + (t + 1) * 128], ident_f[:]),
                    deps=barf + [bank_rd2.get(bi)])
                cp.append(ec.op("v", lambda v, p=p, bank=bank, stg=stg: v.tensor_copy(
                    stg[:, p * 128:(p + 1) * 128], bank[:, 0:128]),
                    deps=[ttok, stage_st], psum=True))
                bank_rd2[bi] = cp[-1]
            st_toks.append(ec.dma("y", "so0", lambda y, stg=stg, t=t: y.dma_start(
                y_out[t * 128:(t + 1) * 128, :], stg), deps=cp))
            stage_st = st_toks[-1]
        ec.raw("y", lambda y: y.nop(), deps=st_toks)

        for e, attr in (("g", "gpsimd"), ("t", "tensor"), ("v", "vector"), ("s", "scalar"), ("y", "sync")):
            steps = ec.plan[e]

            def runner(eng, steps=steps):
                for fstep in steps:
                    fstep(eng)
            getattr(block, attr)(runner)

    return nc


def prep_inputs(inputs, Tc, n_cores=8):
    C = np.ascontiguousarray

    def f32(a):
        return np.asarray(a, dtype=np.float32)
    x = f32(inputs["L0_attn_x"])
    def vT(a):
        a = np.asarray(a)
        return a.reshape(a.shape[:-1] + (a.shape[-1] // 128, 128)).swapaxes(-1, -2)
    w = {
        "Wq": inputs["L0_attn_Wq"], "bq": vT(inputs["L0_attn_bq"]),
        "Wk": inputs["L0_attn_Wk"], "bk": vT(inputs["L0_attn_bk"]),
        "Wv": inputs["L0_attn_Wv"], "bv": vT(inputs["L0_attn_bv"]),
        "bvR": np.asarray(inputs["L0_attn_bv"]).reshape(1, -1),
        "Wo0": inputs["L0_attn_Wo"], "bo0": vT(inputs["L0_attn_bo"]),
        "projT": np.transpose(np.asarray(inputs["L0_attn_projection"]), (0, 2, 1)),
        "Wp1": inputs["L0_ffn_Wp1"], "bp1": inputs["L0_ffn_bp1"],
        "Wp2": inputs["L0_ffn_Wp2"], "bp2": inputs["L0_ffn_bp2"],
        "W1": inputs["L0_ffn_W1"], "b1T": vT(inputs["L0_ffn_b1"]),
        "W2": inputs["L0_ffn_W2"], "b2T": vT(inputs["L0_ffn_b2"]),
        "Wof": inputs["L0_ffn_Wo"], "bof": vT(inputs["L0_ffn_bo"]),
        "ln_w": vT(inputs["L1_attn_ln_w"]), "ln_b": vT(inputs["L1_attn_ln_b"]),
        "A_log": inputs["L1_attn_A_log"], "B_mat": inputs["L1_attn_B_mat"],
        "C_mat": inputs["L1_attn_C_mat"], "D_vec": vT(inputs["L1_attn_D_vec"]),
        "out_wT": np.asarray(inputs["L1_attn_out_w"]).T, "out_b": vT(inputs["L1_attn_out_b"]),
        "ln2_w": vT(inputs["L1_attn_ln2_w"]), "ln2_b": vT(inputs["L1_attn_ln2_b"]),
        "ffn1_wT": np.asarray(inputs["L1_attn_ffn1_w"]).T, "ffn1_b": vT(inputs["L1_attn_ffn1_b"]),
        "ffn2_wT": np.asarray(inputs["L1_attn_ffn2_w"]).T, "ffn2_b": vT(inputs["L1_attn_ffn2_b"]),
        "W_base": inputs["L1_ffn_W_base"], "b_base": vT(inputs["L1_ffn_b_base"]),
        "Wd1": inputs["L1_ffn_Wd1"], "bd1": vT(inputs["L1_ffn_bd1"]),
        "Wd2T": np.transpose(np.asarray(inputs["L1_ffn_Wd2"]), (0, 2, 1)),
        "bd2": vT(inputs["L1_ffn_bd2"]),
        "Wa_down": inputs["L1_ffn_Wa_down"], "ba_down": inputs["L1_ffn_ba_down"],
        "Wa_up": inputs["L1_ffn_Wa_up"], "ba_up": vT(inputs["L1_ffn_ba_up"]),
    }
    w = {k: C(f32(v)) for k, v in w.items()}
    in_maps = []
    for c in range(n_cores):
        b, q = c // (n_cores // 2), c % (n_cores // 2)
        m = dict(w)
        m["x_in"] = C(x[b, q * Tc:(q + 1) * Tc, :])
        cst = np.zeros((128, 160), np.float32)
        cst[:, 0:128] = np.eye(128, dtype=np.float32)
        cst[0, 128:136] = np.arange(8)
        if q > 0:
            cst[:, 141 + q - 1] = 1.0
        cst[0, 140] = q
        m["consts"] = cst
        in_maps.append(m)
    return in_maps


def kernel(**inputs):
    x = np.asarray(inputs["L0_attn_x"])
    B, S, _ = x.shape
    Tc = S // 4
    in_maps = prep_inputs(inputs, Tc)
    nc = build_program(Tc)
    res = run_bass_kernel_spmd(nc, in_maps, core_ids=list(range(8)))
    outs = []
    for b in range(B):
        outs.append(np.concatenate([res.results[b * 4 + q]["y_out"] for q in range(4)], axis=0))
    return np.stack(outs, axis=0).astype(np.float32)


# version 04 (final)
# speedup vs baseline: 1.1363x; 1.1363x over previous
import os
import numpy as np
from contextlib import ExitStack

import concourse.bass as bass
import concourse.mybir as mybir
from concourse.bass_utils import run_bass_kernel_spmd
from concourse.alu_op_type import AluOpType as OP

F32 = mybir.dt.float32
BF16 = mybir.dt.bfloat16
I32 = mybir.dt.int32
AF = mybir.ActivationFunctionType
AX = mybir.AxisListType

D = 1024
PT = D // 128
NH, HD, NF = 16, 64, 256
FF0 = 2048
NE = 8
K_TAPS = 32
H1 = 4096
RAD, HD1 = 64, 256


class EC:
    ENGS = ("t", "v", "s", "g", "y")

    def __init__(self, nc, es):
        self.nc = nc
        self.es = es
        self.plan = {e: [] for e in self.ENGS}
        self.cnt = {e: 0 for e in self.ENGS}
        self.sem = {e: es.enter_context(nc.semaphore(f"sem_{e}")) for e in self.ENGS}
        self.dsem = {}
        self.dcnt = {}
        self.ccsem = es.enter_context(nc.semaphore("ccsem"))
        self.cccnt = 0
        self.wm = {e: {} for e in self.ENGS}
        self.pread = None
        self.bank_reader = {}
        self.tag = "init"

    def _waits(self, e, deps):
        agg = {}
        for dep in deps:
            if dep is None:
                continue
            kind, val = dep
            if kind == e and val >= self.cnt[e] + 1:
                raise RuntimeError(f"self-dep in future: {e} {val} > {self.cnt[e]}")
            if val > agg.get(kind, 0):
                agg[kind] = val
        out = []
        for kind, val in agg.items():
            if val <= 0 or self.wm[e].get(kind, 0) >= val:
                continue
            self.wm[e][kind] = val
            if kind in self.ENGS:
                out.append((self.sem[kind], val))
            elif kind.startswith("D:"):
                out.append((self.dsem[kind[2:]], val))
            elif kind == "cc":
                out.append((self.ccsem, val))
            else:
                raise KeyError(kind)
        return out

    def op(self, e, f, deps=(), n_inc=1, psum=None, wbank=None):
        deps = list(deps)
        if psum is not None:
            if self.pread is not None:
                deps.append(self.pread)
        if wbank is not None:
            wb = wbank if isinstance(wbank, (tuple, list)) else (wbank,)
            for b in wb:
                if self.bank_reader.get(b) is not None:
                    deps.append(self.bank_reader[b])
        waits = self._waits(e, deps)
        self.cnt[e] += n_inc
        sem = self.sem[e]

        def step(eng, waits=waits, f=f, sem=sem, n_inc=n_inc):
            for s, v in waits:
                eng.wait_ge(s, v)
            f(eng).then_inc(sem, n_inc)
        self.plan[e].append((self.tag, step))
        tok = (e, self.cnt[e])
        if psum is not None:
            self.pread = tok
            pbk = psum if isinstance(psum, (tuple, list)) else (psum,)
            for b in pbk:
                if b is not True:
                    self.bank_reader[b] = tok
        return tok

    def raw(self, e, f, deps=()):
        waits = self._waits(e, list(deps))

        def step(eng, waits=waits, f=f):
            for s, v in waits:
                eng.wait_ge(s, v)
            f(eng)
        self.plan[e].append((self.tag, step))

    def dma(self, e, key, f, deps=()):
        if key not in self.dsem:
            self.dsem[key] = self.es.enter_context(self.nc.semaphore("dk_" + key))
            self.dcnt[key] = 0
        deps = list(deps)
        if e == "g" and self.dcnt[key] >= 16:
            deps.append(("D:" + key, self.dcnt[key]))
        waits = self._waits(e, deps)
        self.dcnt[key] += 16
        sem = self.dsem[key]

        def step(eng, waits=waits, f=f, sem=sem):
            for s, v in waits:
                eng.wait_ge(s, v)
            f(eng).then_inc(sem, 16)
        self.plan[e].append((self.tag, step))
        return ("D:" + key, self.dcnt[key])

    def cc(self, f, deps=()):
        waits = self._waits("g", list(deps))
        self.cccnt += 1
        sem = self.ccsem

        def step(eng, waits=waits, f=f, sem=sem):
            for s, v in waits:
                eng.wait_ge(s, v)
            f(eng).then_inc(sem, 1)
        self.plan["g"].append((self.tag, step))
        return ("cc", self.cccnt)

    def bar(self):
        toks = [(e, self.cnt[e]) for e in self.ENGS if self.cnt[e] > 0]
        toks += [("D:" + k, v) for k, v in self.dcnt.items() if v > 0]
        if self.cccnt:
            toks.append(("cc", self.cccnt))
        return toks


def build_program(Tc, n_cores=8, enable=(1, 2, 3, 4, 5, 6)):
    TCW = 512 if Tc % 512 == 0 else 128
    NTC = Tc // TCW
    NKW = TCW // 128
    nc = bass.Bass("TRN2", target_bir_lowering=False)
    gsz = n_cores // 2
    groups = [list(range(gsz)), list(range(gsz, n_cores))]

    x_in = nc.dram_tensor("x_in", [Tc, D], F32, kind="ExternalInput").ap()
    y_out = nc.dram_tensor("y_out", [Tc, D], F32, kind="ExternalOutput").ap()
    consts = nc.dram_tensor("consts", [128, 160], F32, kind="ExternalInput").ap()

    W = {}
    def win(name, shape):
        W[name] = nc.dram_tensor(name, shape, F32, kind="ExternalInput").ap()
    VPP = [128, PT]
    for nm, sh in [("Wq", [D, D]), ("bq", VPP), ("Wk", [D, D]), ("bk", VPP),
                   ("Wv", [D, D]), ("bv", VPP), ("bvR", [1, D]), ("Wo0", [D, D]), ("bo0", VPP),
                   ("projT", [NH, HD, NF]),
                   ("Wp1", [D, 128]), ("bp1", [128]), ("Wp2", [128, NE]), ("bp2", [NE]),
                   ("W1", [NE, D, FF0]), ("b1T", [NE, 128, FF0 // 128]),
                   ("W2", [NE, FF0, D]), ("b2T", [NE, 128, D // 128]),
                   ("Wof", [D, D]), ("bof", VPP),
                   ("ln_w", VPP), ("ln_b", VPP), ("A_log", [D, 16]), ("B_mat", [D, 16]),
                   ("C_mat", [D, 16]), ("D_vec", VPP), ("out_wT", [D, D]), ("out_b", VPP),
                   ("ln2_w", VPP), ("ln2_b", VPP), ("ffn1_wT", [D, H1]), ("ffn1_b", [128, H1 // 128]),
                   ("ffn2_wT", [H1, D]), ("ffn2_b", VPP),
                   ("W_base", [D, D]), ("b_base", VPP),
                   ("Wd1", [NE, D, HD1]), ("bd1", [NE, 128, HD1 // 128]),
                   ("Wd2T", [NE, D, HD1]), ("bd2", [NE, 128, D // 128]),
                   ("Wa_down", [NE, D, RAD]), ("ba_down", [NE, RAD]),
                   ("Wa_up", [NE, RAD, D]), ("ba_up", [NE, 128, D // 128])]:
        win(nm, sh)

    arena = {}
    off = 0
    for nm, n in [("Wq", D * D), ("Wk", D * D), ("Wv", D * D), ("Wo0", D * D),
                  ("Wof", D * D), ("out_wT", D * D), ("W_base", D * D),
                  ("ffn1_wT", D * H1), ("ffn2_wT", H1 * D),
                  ("Wd1", NE * D * HD1), ("Wd2T", NE * D * HD1),
                  ("projT", NH * HD * NF),
                  ("W1s1", D * FF0), ("W2s1", FF0 * D),
                  ("W1s3", D * FF0), ("W2s3", FF0 * D)]:
        arena[nm] = off
        off += n
    wscr = nc.dram_tensor("wscr", [off], BF16).ap()
    cc_in = nc.dram_tensor("cc_in", [128 * 2112], F32).ap()
    cc_out = nc.dram_tensor("cc_out", [128 * 2112], F32).ap()
    agr_in = nc.dram_tensor("agr_in", [1024], F32).ap()
    ag_out = nc.dram_tensor("ag_out", [gsz * 1024], F32).ap()
    h_scr = nc.dram_tensor("h_scr", [2 * D * Tc], BF16).ap()
    hal_in = nc.dram_tensor("hal_in", [D * 32], BF16).ap()
    hal_out = nc.dram_tensor("hal_out", [gsz * D * 32], BF16).ap()

    es = ExitStack()
    with es:
        ec = EC(nc, es)

        def sb(name, shape, dt=F32):
            return es.enter_context(nc.sbuf_tensor(name, shape, dt))

        X = sb("X", [128, PT * Tc], F32)
        LNDT = F32 if os.environ.get("K_LNF32") else BF16
        mean_t = sb("mean_t", [128, Tc], LNDT)
        rstd_t = sb("rstd_t", [128, Tc], LNDT)
        poolA = sb("poolA", [128, 16 * 512], BF16)
        poolB = sb("poolB", [128, 8 * 512], BF16)
        poolC = sb("poolC", [128, 8 * 512], BF16)
        stage = sb("stage", [128, 2 * D], F32)
        sw = sb("sw", [128, 3 * D], BF16)
        dbuf = sb("dbuf", [128, 512], F32)
        hbuf = sb("hbuf", [128, 512], F32)
        varb = sb("varb", [128, 512], F32)
        msq = sb("msq", [128, 512], F32)
        ones_b = sb("ones_b", [128, 128], BF16)
        ones_f = sb("ones_f", [128, 128], F32)
        ident_f = sb("ident_f", [128, 128], F32)
        eps_t = sb("eps_t", [128, 1], F32)
        eps8_t = sb("eps8_t", [128, 1], F32)
        ident_b = sb("ident_b", [128, 128], BF16)
        projs = sb("projs", [128, 8 * NF], BF16)
        bones_f = sb("bones_f", [128, 128], F32)
        gbuf = sb("gbuf", [128, 512], F32)

        biasp = sb("biasp", [128, 108], F32)
        smallp = sb("smallp", [128, 216], F32)
        idxi = sb("idxi", [1, 1], I32)
        iot8 = sb("iot8", [1, 8], F32)
        pb = [es.enter_context(nc.psum_tensor(f"pb{i}", [128, 512], F32)) for i in range(8)]

        selreg = es.enter_context(nc.gpsimd.register("selreg"))
        offreg = es.enter_context(nc.gpsimd.register("offreg"))

        block = es.enter_context(nc.Block())

        ec.op("g", lambda g: g.memset(ones_b[:], 1.0))
        ec.op("g", lambda g: g.memset(ones_f[:], 1.0))
        ec.op("g", lambda g: g.memset(eps_t[:], 1e-5))
        ec.op("g", lambda g: g.memset(eps8_t[:], 1e-8))
        t_bz = ec.op("g", lambda g: g.memset(bones_f[:], 0.0))
        t_b1 = ec.op("g", lambda g: g.memset(bones_f[0:64, 0:64], 1.0), deps=[t_bz])
        ec.op("g", lambda g: g.memset(bones_f[64:128, 64:128], 1.0), deps=[t_b1])
        d_c = ec.dma("y", "mcst", lambda y: y.dma_start(
            ident_f[:], bass.AP(consts.tensor, 0, [[160, 128], [1, 128]])))
        d_ci = ec.dma("y", "mcio", lambda y: y.dma_start(
            iot8[0:1, 0:8], bass.AP(consts.tensor, 128, [[160, 1], [1, 8]])))
        d_cm = ec.dma("y", "mcmk", lambda y: y.dma_start(
            biasp[:, 104:108], bass.AP(consts.tensor, 141, [[160, 128], [1, 4]])))
        t_id = ec.op("v", lambda v: v.tensor_copy(ident_b[:], ident_f[:]), deps=[d_c])
        t_idb = t_id
        tok_init = ec.bar()

        def cast_to_arena(nm_dst, hbm, row0, nrows, ncols, deps):
            if os.environ.get("K_NOCAST"):
                return list(deps)
            toks = []
            base = arena[nm_dst]
            step = max(1, (2 ** 21) // ncols)
            for r0 in range(0, nrows, step):
                nr = min(step, nrows - r0)
                toks.append(ec.dma("g", "aro_" + nm_dst,
                            lambda g, r0=r0, nr=nr: g.dma_start(
                    bass.AP(wscr.tensor, base + r0 * ncols, [[ncols, nr], [1, ncols]]),
                    bass.AP(hbm.tensor, (row0 + r0) * ncols, [[ncols, nr], [1, ncols]])),
                    deps=list(deps)))
            return toks

        ec.tag = "io_in"
        NT128 = Tc // 128
        bank_rd = {}
        stage_rd = None
        for t in range(NT128 if not os.environ.get("K_NOIN") else 0):
            stg = stage[:, 0:D]
            dep_prev = [stage_rd] if stage_rd else list(tok_init)
            dtok = ec.dma("y", "io0", lambda y, stg=stg, t=t: y.dma_start(
                stg, x_in[t * 128:(t + 1) * 128, :]), deps=dep_prev)
            for p in range(PT):
                bi = p % 4
                bank = pb[bi]
                ttok = ec.op("t", lambda te, stg=stg, p=p, bank=bank: te.transpose(
                    bank[:, 0:128], stg[:, p * 128:(p + 1) * 128], ident_f[:]),
                    deps=[dtok, t_id], wbank=bi)
                vtok = ec.op("v", lambda v, p=p, t=t, bank=bank: v.tensor_copy(
                    X[:, p * Tc + t * 128: p * Tc + (t + 1) * 128], bank[:, 0:128]),
                    deps=[ttok], psum=bi)
                bank_rd[bi] = vtok
                if p == PT - 1:
                    stage_rd = ttok
        tok_x = ec.bar()

        ec.tag = "cast"
        arena_tok = {}
        arena_tok["Wq"] = cast_to_arena("Wq", W["Wq"], 0, D, D, tok_init)
        arena_tok["Wk"] = cast_to_arena("Wk", W["Wk"], 0, D, D, tok_init)
        arena_tok["Wv"] = cast_to_arena("Wv", W["Wv"], 0, D, D, tok_init)
        arena_tok["Wo0"] = cast_to_arena("Wo0", W["Wo0"], 0, D, D, tok_init)
        arena_tok["Wof"] = cast_to_arena("Wof", W["Wof"], 0, D, D, tok_init)
        arena_tok["out_wT"] = cast_to_arena("out_wT", W["out_wT"], 0, D, D, tok_init)
        arena_tok["W_base"] = cast_to_arena("W_base", W["W_base"], 0, D, D, tok_init)
        arena_tok["ffn1_wT"] = cast_to_arena("ffn1_wT", W["ffn1_wT"], 0, D, H1, tok_init)
        arena_tok["ffn2_wT"] = cast_to_arena("ffn2_wT", W["ffn2_wT"], 0, H1, D, tok_init)
        arena_tok["Wd1"] = cast_to_arena("Wd1", W["Wd1"], 0, NE * D, HD1, tok_init)
        arena_tok["Wd2T"] = cast_to_arena("Wd2T", W["Wd2T"], 0, NE * D, HD1, tok_init)
        arena_tok["projT"] = cast_to_arena("projT", W["projT"], 0, NH * HD, NF, tok_init)

        def emit_ln(deps):
            toks = []
            for tc in range(NTC):
                c0 = tc * TCW
                pA, pB = pb[4], pb[5]
                sq_toks = []
                for p in range(PT):
                    sq_toks.append(ec.op("s", lambda s, p=p, c0=c0: s.activation(
                        poolA[:, p * 512: p * 512 + TCW], X[:, p * Tc + c0: p * Tc + c0 + TCW],
                        AF.Square, bias=0.0, scale=1.0), deps=deps + [("t", ec.cnt["t"])]))
                mm = mm2 = None
                for p in range(PT):
                    mm = ec.op("t", lambda te, p=p, c0=c0, pA=pA: te.matmul(
                        pA[:, 0:TCW], ones_f[:, 0:128], X[:, p * Tc + c0: p * Tc + c0 + TCW],
                        start=(p == 0), stop=(p == PT - 1)),
                        deps=deps, wbank=4 if p == 0 else None)
                for p in range(PT):
                    mm2 = ec.op("t", lambda te, p=p, pB=pB: te.matmul(
                        pB[:, 0:TCW], ones_b[:, 0:128], poolA[:, p * 512: p * 512 + TCW],
                        start=(p == 0), stop=(p == PT - 1)),
                        deps=[sq_toks[p]], wbank=5 if p == 0 else None)
                t_mean = ec.op("v", lambda v, c0=c0, pA=pA: v.tensor_scalar(
                    mean_t[:, c0:c0 + TCW], pA[:, 0:TCW], 1.0 / D, None, OP.mult, OP.bypass),
                    deps=[mm], psum=4)
                t_msq = ec.op("s", lambda s, pA=pA: s.activation(
                    msq[:, 0:TCW], pA[:, 0:TCW], AF.Square, bias=0.0, scale=1.0 / D),
                    deps=[mm], psum=4)
                t_var = ec.op("v", lambda v, pB=pB: v.scalar_tensor_tensor(
                    varb[:, 0:TCW], pB[:, 0:TCW], 1.0 / D, msq[:, 0:TCW], OP.mult, OP.subtract),
                    deps=[mm2, t_msq, t_mean], psum=5)
                t_ln = ec.op("s", lambda s: s.activation(
                    msq[:, 0:TCW], varb[:, 0:TCW], AF.Ln, bias=eps_t[:], scale=1.0),
                    deps=[t_var])
                t_r = ec.op("s", lambda s, c0=c0: s.activation(
                    rstd_t[:, c0:c0 + TCW], msq[:, 0:TCW], AF.Exp, bias=0.0, scale=-0.5),
                    deps=[t_ln])
                toks += [t_mean, t_r]
            return toks

        def emit_h_chunk(tc, p, dep, add_to_x=0.0, dst=None, dst_col=0):
            c0 = tc * TCW
            if dst is None:
                dst, dst_col = poolB, p * 512
            t_d = ec.op("v", lambda v, p=p, c0=c0: v.tensor_tensor(
                dbuf[:, 0:TCW], X[:, p * Tc + c0:p * Tc + c0 + TCW], mean_t[:, c0:c0 + TCW],
                OP.subtract), deps=list(dep) + [("v", ec.cnt["v"])])
            if add_to_x:
                t_h32 = ec.op("v", lambda v, c0=c0: v.tensor_tensor(
                    hbuf[:, 0:TCW], dbuf[:, 0:TCW], rstd_t[:, c0:c0 + TCW], OP.mult), deps=[t_d])
                t_h = ec.op("v", lambda v, dst=dst, dst_col=dst_col: v.tensor_copy(
                    dst[:, dst_col:dst_col + TCW], hbuf[:, 0:TCW]), deps=[t_h32])
                t_x = ec.op("v", lambda v, p=p, c0=c0, a=add_to_x: v.scalar_tensor_tensor(
                    X[:, p * Tc + c0:p * Tc + c0 + TCW], hbuf[:, 0:TCW], a,
                    X[:, p * Tc + c0:p * Tc + c0 + TCW], OP.mult, OP.add), deps=[t_h32])
                return t_h, t_x
            t_h = ec.op("v", lambda v, c0=c0, dst=dst, dst_col=dst_col: v.tensor_tensor(
                dst[:, dst_col:dst_col + TCW], dbuf[:, 0:TCW], rstd_t[:, c0:c0 + TCW], OP.mult),
                deps=[t_d])
            return t_h, None

        class WStream:
            NSLOT = 3

            def __init__(self):
                self.i = 0
                self.cons = {}

            def load(self, arena_off, ncols=D, row_stride=None, extra_deps=(), colblock=None):
                slot = self.i % self.NSLOT
                self.i += 1
                dep = [self.cons.get(slot)] if slot in self.cons else []
                if colblock is not None:
                    trs, bc = colblock
                    tok = ec.dma("y", f"w{slot}",
                                 lambda y, slot=slot, arena_off=arena_off, trs=trs, bc=bc: y.dma_start(
                        sw[:, slot * D: slot * D + PT * bc],
                        bass.AP(wscr.tensor, arena_off, [[trs, 128], [128 * trs, PT], [1, bc]])),
                        deps=dep + list(extra_deps))
                    return slot, tok
                rs = row_stride if row_stride is not None else ncols
                tok = ec.dma("y", f"w{slot}",
                             lambda y, slot=slot, arena_off=arena_off, ncols=ncols, rs=rs: y.dma_start(
                    sw[:, slot * D: slot * D + ncols],
                    bass.AP(wscr.tensor, arena_off, [[rs, 128], [1, ncols]])),
                    deps=dep + list(extra_deps))
                return slot, tok

            def done(self, slot, tok):
                self.cons[slot] = tok

        ws = WStream()

        def emit_l0ffn(tag):
            ec.tag = f"s{tag}"
            arena_w1 = arena["W1s1" if tag == 1 else "W1s3"]
            arena_w2 = arena["W2s1" if tag == 1 else "W2s3"]
            ln_toks = emit_ln(ec.bar())
            bar0 = ec.bar()
            hcol = []
            for p in range(PT):
                t_d = ec.op("v", lambda v, p=p: v.tensor_tensor(
                    smallp[:, 64 + p:64 + p + 1], X[:, p * Tc + Tc - 1: p * Tc + Tc],
                    mean_t[:, Tc - 1:Tc], OP.subtract), deps=bar0)
                hcol.append(ec.op("v", lambda v, p=p: v.tensor_tensor(
                    smallp[:, 72 + p:72 + p + 1], smallp[:, 64 + p:64 + p + 1],
                    rstd_t[:, Tc - 1:Tc], OP.mult), deps=[t_d]))
            d1 = ec.dma("g", "ccio", lambda g: g.dma_start(
                bass.AP(agr_in.tensor, 0, [[PT, 128], [1, PT]]), smallp[:, 72:72 + PT]), deps=hcol)
            cctok = ec.cc(lambda g: g.collective_compute(
                "AllGather", OP.bypass, replica_groups=groups,
                ins=[bass.AP(agr_in.tensor, 0, [[1, 1024]]).opt()],
                outs=[bass.AP(ag_out.tensor, 0, [[1, gsz * 1024]]).opt()]), deps=[d1])
            d2 = ec.dma("g", "ccio", lambda g: g.dma_start(
                smallp[:, 80:80 + PT],
                bass.AP(ag_out.tensor, (gsz - 1) * 1024, [[PT, 128], [1, PT]])), deps=[cctok])
            dw1 = []
            for k in range(8):
                dw1.append(ec.dma("y", "mwp1", lambda y, k=k: y.dma_start(
                    stage[:, k * 128:(k + 1) * 128],
                    bass.AP(W["Wp1"].tensor, k * 128 * 128, [[128, 128], [1, 128]])),
                    deps=bar0))
            dW2 = ec.dma("y", "mwp2", lambda y: y.dma_start(smallp[:, 88:88 + NE], W["Wp2"]), deps=bar0)
            dB1 = ec.dma("y", "mbp1", lambda y: y.dma_start(
                biasp[:, 0:1], bass.AP(W["bp1"].tensor, 0, [[1, 128], [1, 1]])), deps=bar0)
            dB2 = ec.dma("y", "mbp2", lambda y: y.dma_start(
                msq[0:1, 200:200 + NE], bass.AP(W["bp2"].tensor, 0, [[0, 1], [1, NE]])), deps=bar0)
            mmr = None
            for k in range(8):
                mmr = ec.op("t", lambda te, k=k: te.matmul(
                    pb[6][:, 0:1], stage[:, k * 128:(k + 1) * 128], smallp[:, 80 + k:80 + k + 1],
                    start=(k == 0), stop=(k == 7)),
                    deps=[d2] + dw1, wbank=6 if k == 0 else None)
            t_g1 = ec.op("s", lambda s: s.activation(
                smallp[:, 100:101], pb[6][:, 0:1], AF.Gelu, bias=biasp[:, 0:1], scale=1.0),
                deps=[mmr, dB1], psum=6)
            mm2r = ec.op("t", lambda te: te.matmul(
                pb[7][0:1, 0:NE], smallp[:, 100:101], smallp[:, 88:88 + NE], start=True, stop=True),
                deps=[t_g1, dW2], wbank=7)
            t_lg = ec.op("v", lambda v: v.tensor_tensor(
                msq[0:1, 192:192 + NE], pb[7][0:1, 0:NE], msq[0:1, 200:200 + NE], OP.add), deps=[mm2r, dB2], psum=7)
            t_mx = ec.op("v", lambda v: v.tensor_reduce(
                msq[0:1, 208:209], msq[0:1, 192:192 + NE], axis=AX.X, op=OP.max), deps=[t_lg])
            t_oh = ec.op("v", lambda v: v.tensor_scalar(
                msq[0:1, 216:216 + NE], msq[0:1, 192:192 + NE], msq[0:1, 208:209], None, OP.is_ge, OP.bypass),
                deps=[t_mx])
            t_ix0 = ec.op("v", lambda v: v.tensor_tensor(
                msq[0:1, 216:216 + NE], msq[0:1, 216:216 + NE], iot8[0:1, 0:8], OP.mult),
                deps=[t_oh, d_ci])
            t_ix = ec.op("v", lambda v: v.tensor_scalar(
                varb[0:1, 0:NE], msq[0:1, 216:216 + NE], 1.0, 0.0, OP.mult, OP.add,
                accum_out=msq[0:1, 224:225]), deps=[t_ix0])
            t_ii = ec.op("v", lambda v: v.tensor_copy(idxi[:], msq[0:1, 224:225]), deps=[t_ix])
            ec.raw("g", lambda g: g.reg_load(selreg, idxi[0:1, 0:1]), deps=[t_ii])
            t_selld = ("v", ec.cnt["v"])

            gkey = f"gw{tag}"
            gdeps = ec._waits("g", bar0 + [t_selld])

            def _gather_branches(g, gdeps=gdeps):
                for s_, v_ in gdeps:
                    g.wait_ge(s_, v_)
                for e in range(NE):
                    with g.If_eq(selreg, e):
                        g.dma_start(
                            bass.AP(wscr.tensor, arena_w1, [[FF0, D], [1, FF0]]),
                            bass.AP(W["W1"].tensor, e * D * FF0, [[FF0, D], [1, FF0]])
                        ).then_inc(ec.dsem[gkey], 16)
                        g.dma_start(
                            bass.AP(wscr.tensor, arena_w2, [[D, FF0], [1, D]]),
                            bass.AP(W["W2"].tensor, e * FF0 * D, [[D, FF0], [1, D]])
                        ).then_inc(ec.dsem[gkey], 16)
                        g.dma_start(biasp[:, 0:16],
                                    bass.AP(W["b1T"].tensor, e * FF0, [[16, 128], [1, 16]])
                                    ).then_inc(ec.dsem[gkey], 16)
                        g.dma_start(biasp[:, 16:24],
                                    bass.AP(W["b2T"].tensor, e * D, [[8, 128], [1, 8]])
                                    ).then_inc(ec.dsem[gkey], 16)

            if gkey not in ec.dsem:
                ec.dsem[gkey] = es.enter_context(nc.semaphore("dk_" + gkey))
                ec.dcnt[gkey] = 0
            ec.dcnt[gkey] += 64
            ec.plan["g"].append((ec.tag, _gather_branches))
            gath = [("D:" + gkey, ec.dcnt[gkey])]
            d_b1 = d_b2 = gath[0]
            d_bo = ec.dma("y", "mbo", lambda y: y.dma_start(
                biasp[:, 24:32], W["bof"]), deps=bar0)
            t_bo = ec.op("v", lambda v: v.tensor_scalar(
                biasp[:, 32:40], biasp[:, 24:32], 0.15, None, OP.mult, OP.bypass), deps=[d_bo])

            wof_deps = arena_tok["Wof"]
            for tc in range(NTC):
                h_toks = []
                for p in range(PT):
                    th, _ = emit_h_chunk(tc, p, ln_toks, add_to_x=0.5)
                    h_toks.append(th)
                g_toks = []
                for half in range(2):
                    mm_last = [None] * 8
                    for k in range(PT):
                        slot, wtok = ws.load(arena_w1 + k * 128 * FF0 + half * 1024, row_stride=FF0, extra_deps=gath)
                        for m in range(8):
                            mm_last[m] = ec.op("t", lambda te, k=k, m=m, slot=slot: te.matmul(
                                pb[m][:, 0:TCW], sw[:, slot * D + m * 128: slot * D + (m + 1) * 128],
                                poolB[:, k * 512: k * 512 + TCW],
                                start=(k == 0), stop=(k == PT - 1)),
                                deps=[wtok, h_toks[k]], wbank=m if k == 0 else None)
                        ws.done(slot, mm_last[7])
                    for m in range(8):
                        mc = half * 8 + m
                        g_toks.append(ec.op("s", lambda s, m=m, mc=mc: s.activation(
                            poolA[:, mc * 512: mc * 512 + TCW], pb[m][:, 0:TCW], AF.Gelu,
                            bias=biasp[:, mc:mc + 1], scale=1.0),
                            deps=[mm_last[m], d_b1], psum=m))
                mm_last = [None] * 8
                for k2 in range(16):
                    slot, wtok = ws.load(arena_w2 + k2 * 128 * D, extra_deps=gath)
                    for mo in range(8):
                        mm_last[mo] = ec.op("t", lambda te, k2=k2, mo=mo, slot=slot: te.matmul(
                            pb[mo][:, 0:TCW], sw[:, slot * D + mo * 128: slot * D + (mo + 1) * 128],
                            poolA[:, k2 * 512: k2 * 512 + TCW],
                            start=(k2 == 0), stop=(k2 == 15)),
                            deps=[wtok, g_toks[k2]], wbank=mo if k2 == 0 else None)
                    ws.done(slot, mm_last[7])
                r_toks = []
                for mo in range(8):
                    r_toks.append(ec.op("s", lambda s, mo=mo: s.activation(
                        poolC[:, mo * 512: mo * 512 + TCW], pb[mo][:, 0:TCW], AF.Identity,
                        bias=biasp[:, 16 + mo:16 + mo + 1], scale=1.0),
                        deps=[mm_last[mo], d_b2], psum=mo))
                mm_last = [None] * 8
                for k3 in range(PT):
                    slot, wtok = ws.load(arena["Wof"] + k3 * 128 * D, extra_deps=wof_deps)
                    for m3 in range(8):
                        mm_last[m3] = ec.op("t", lambda te, k3=k3, m3=m3, slot=slot: te.matmul(
                            pb[m3][:, 0:TCW], sw[:, slot * D + m3 * 128: slot * D + (m3 + 1) * 128],
                            poolC[:, k3 * 512: k3 * 512 + TCW],
                            start=(k3 == 0), stop=(k3 == PT - 1)),
                            deps=[wtok, r_toks[k3]], wbank=m3 if k3 == 0 else None)
                    ws.done(slot, mm_last[7])
                for m3 in range(8):
                    c0 = tc * TCW
                    t_u = ec.op("v", lambda v, m3=m3: v.tensor_scalar(
                        dbuf[:, 0:TCW], pb[m3][:, 0:TCW], biasp[:, 24 + m3:24 + m3 + 1], 0.15,
                        OP.add, OP.mult), deps=[mm_last[m3], d_bo, ("v", ec.cnt["v"])], psum=m3)
                    ec.op("v", lambda v, m3=m3, c0=c0: v.tensor_tensor(
                        X[:, m3 * Tc + c0: m3 * Tc + c0 + TCW],
                        X[:, m3 * Tc + c0: m3 * Tc + c0 + TCW], dbuf[:, 0:TCW], OP.add),
                        deps=[t_u])


        def emit_l0attn():
            ec.tag = "s2.ln"
            lvl = int(os.environ.get("K_STOP", "99"))
            Tglob = 4 * Tc
            if lvl == 0:
                return
            if lvl != 11:
                ln_toks = emit_ln(ec.bar())
            else:
                ln_toks = ec.bar()
            bar0 = ec.bar()
            if lvl != 12:
                for h in range(NH):
                    ec.dma("y", "mproj", lambda y, h=h: y.dma_start(
                        projs[(h % 2) * 64:(h % 2) * 64 + 64, (h // 2) * NF:(h // 2) * NF + NF],
                        bass.AP(wscr.tensor, arena["projT"] + h * HD * NF, [[NF, 64], [1, NF]])),
                        deps=arena_tok["projT"] + (bar0 if h == 0 else []))
            dproj = ("D:mproj", ec.dcnt["mproj"]) if lvl != 12 else None
            d_bq = ec.dma("y", "mbq", lambda y: y.dma_start(biasp[:, 40:48], W["bq"]), deps=bar0)
            d_bk = ec.dma("y", "mbk", lambda y: y.dma_start(biasp[:, 48:56], W["bk"]), deps=bar0)
            d_bv = ec.dma("y", "mbv", lambda y: y.dma_start(biasp[:, 56:64], W["bv"]), deps=bar0)
            d_bo0 = ec.dma("y", "mbo0", lambda y: y.dma_start(biasp[:, 64:72], W["bo0"]), deps=bar0)
            if lvl in (11, 12):
                return

            def proj_norm_chunk(tc, wname, bcol, h_toks, bdep):
                kn_toks = [None] * PT
                for m in range(PT):
                    slot, wtok = ws.load(arena[wname] + m * 128, extra_deps=arena_tok[wname],
                                         colblock=(D, 128))
                    bank = pb[m % 2]
                    mmk = None
                    for k in range(PT):
                        mmk = ec.op("t", lambda te, k=k, bank=bank, slot=slot: te.matmul(
                            bank[:, 0:TCW], sw[:, slot * D + k * 128: slot * D + (k + 1) * 128],
                            poolB[:, k * 512: k * 512 + TCW], start=(k == 0), stop=(k == PT - 1)),
                            deps=[wtok, h_toks[k]], wbank=(m % 2) if k == 0 else None)
                    ws.done(slot, mmk)
                    t_sq = ec.op("s", lambda s, m=m, bank=bank: s.activation(
                        stage[:, 0:TCW], bank[:, 0:TCW], AF.Square,
                        bias=biasp[:, bcol + m:bcol + m + 1], scale=1.0),
                        deps=[mmk, bdep, ("t", ec.cnt["t"]), ("v", ec.cnt["v"])], psum=m % 2)
                    mmn = ec.op("t", lambda te: te.matmul(
                        pb[2][:, 0:TCW], bones_f[:], stage[:, 0:TCW], start=True, stop=True),
                        deps=[t_sq], wbank=2)
                    t_r1 = ec.op("s", lambda s: s.activation(
                        msq[:, 0:TCW], pb[2][:, 0:TCW], AF.Sqrt, bias=0.0, scale=1.0),
                        deps=[mmn, ("s", ec.cnt["s"])], psum=2)
                    t_r2 = ec.op("s", lambda s: s.activation(
                        msq[:, 0:TCW], msq[:, 0:TCW], AF.Ln, bias=eps8_t[:], scale=1.0), deps=[t_r1])
                    t_r3 = ec.op("s", lambda s: s.activation(
                        varb[:, 0:TCW], msq[:, 0:TCW], AF.Exp, bias=0.0, scale=-1.0), deps=[t_r2])
                    kn_toks[m] = ec.op("v", lambda v, m=m, bank=bank: v.scalar_tensor_tensor(
                        poolA[:, m * 512: m * 512 + TCW], bank[:, 0:TCW],
                        biasp[:, bcol + m:bcol + m + 1], varb[:, 0:TCW], OP.add, OP.mult),
                        deps=[t_r3, ("v", ec.cnt["v"])], psum=m % 2)
                return kn_toks

            ec.tag = "s2.ph1"
            if lvl <= 1:
                return
            bank_first = {3: True, 4: True, 5: True, 6: True, 7: True}
            for tc in range(NTC):
                tdep = [("t", ec.cnt["t"])]
                h_toks = []
                for p in range(PT):
                    th, _ = emit_h_chunk(tc, p, ln_toks + tdep + (bar0 if tc == 0 else []))
                    h_toks.append(th)
                kn_toks = proj_norm_chunk(tc, "Wk", 48, h_toks, d_bk)
                if lvl <= 2:
                    continue
                vt_toks = [None] * (NKW * PT)
                for qt in range(4):
                    ev_toks = []
                    for m2 in range(2):
                        m = qt * 2 + m2
                        slot, wtok = ws.load(arena["Wv"] + m * 128, extra_deps=arena_tok["Wv"],
                                             colblock=(D, 128))
                        bank = pb[m2]
                        mmv = None
                        for k in range(PT):
                            mmv = ec.op("t", lambda te, k=k, bank=bank, slot=slot: te.matmul(
                                bank[:, 0:TCW], sw[:, slot * D + k * 128: slot * D + (k + 1) * 128],
                                poolB[:, k * 512: k * 512 + TCW], start=(k == 0), stop=(k == PT - 1)),
                                deps=[wtok, h_toks[k]], wbank=m2 if k == 0 else None)
                        ws.done(slot, mmv)
                        ev_toks.append(ec.op("s", lambda s, m2=m2, m=m, bank=bank: s.activation(
                            stage[:, m2 * 512: m2 * 512 + TCW], bank[:, 0:TCW], AF.Identity,
                            bias=biasp[:, 56 + m:56 + m + 1], scale=1.0),
                            deps=[mmv, d_bv, ("t", ec.cnt["t"])], psum=m2))
                    for m2 in range(2):
                        m = qt * 2 + m2
                        for sub in range(NKW):
                            mmt = ec.op("t", lambda te, m2=m2, sub=sub: te.transpose(
                                pb[2][:, (sub % 4) * 128:(sub % 4) * 128 + 128],
                                stage[:, m2 * 512 + sub * 128: m2 * 512 + sub * 128 + 128],
                                ident_f[:]),
                                deps=[ev_toks[m2], t_id], wbank=2)
                            vt_toks[sub * PT + m] = ec.op("s", lambda s, m=m, sub=sub: s.activation(
                                poolC[:, sub * 1024 + m * 128: sub * 1024 + (m + 1) * 128],
                                pb[2][:, (sub % 4) * 128:(sub % 4) * 128 + 128],
                                AF.Identity, bias=0.0, scale=1.0), deps=[mmt], psum=2)
                if lvl <= 3:
                    continue
                for sub in range(NKW):
                    vdep = [vt_toks[sub * PT + m] for m in range(PT)]
                    for h in range(NH):
                        po = (h % 2) * 64
                        mmkp = ec.op("t", lambda te, h=h, sub=sub, po=po: te.matmul(
                            pb[2][:, 0:NF],
                            poolA[po:po + 64, (h // 2) * 512 + sub * 128: (h // 2) * 512 + sub * 128 + 128],
                            projs[po:po + 64, (h // 2) * NF: (h // 2) * NF + NF],
                            start=True, stop=True),
                            deps=[kn_toks[h // 2], dproj], wbank=2)
                        t_kp = ec.op("s", lambda s: s.activation(
                            poolA[:, 4096: 4096 + NF], pb[2][:, 0:NF], AF.Relu, bias=0.0, scale=1.0),
                            deps=[mmkp, ("t", ec.cnt["t"])], psum=2)
                        for fc in range(2):
                            sidx = h * 2 + fc
                            bank = pb[4 + sidx // 8]
                            col = (sidx % 8) * 64
                            stf = bank_first[4 + sidx // 8] and tc == 0 and sub == 0
                            if stf:
                                bank_first[4 + sidx // 8] = False
                            fin = tc == NTC - 1 and sub == NKW - 1 and sidx % 8 == 7
                            ec.op("t", lambda te, h=h, sub=sub, fc=fc, bank=bank, col=col, st=stf,
                                  fin=fin: te.matmul(
                                bank[:, col:col + 64], poolA[:, 4096 + fc * 128: 4096 + (fc + 1) * 128],
                                poolC[:, sub * 1024 + h * 64: sub * 1024 + h * 64 + 64],
                                start=st, stop=fin, skip_group_check=not (st or fin)),
                                deps=[t_kp] + vdep, wbank=(4 + sidx // 8) if stf else None)
                            st2 = bank_first[3] and tc == 0 and sub == 0
                            if st2:
                                bank_first[3] = False
                            ec.op("t", lambda te, fc=fc, sidx=sidx, st=st2: te.matmul(
                                pb[3][:, sidx:sidx + 1], poolA[:, 4096 + fc * 128: 4096 + (fc + 1) * 128],
                                ones_b[:, 0:1], start=st, stop=False, skip_group_check=not st),
                                deps=[t_kp], wbank=3 if st2 else None)
                    for pr in range(PT):
                        fin3 = tc == NTC - 1 and sub == NKW - 1 and pr == PT - 1
                        ec.op("t", lambda te, pr=pr, sub=sub, fin3=fin3: te.matmul(
                            pb[3][:, 32 + pr:32 + pr + 1],
                            poolC[:, sub * 1024 + pr * 128: sub * 1024 + (pr + 1) * 128],
                            ones_b[:, 0:1], start=False, stop=fin3,
                            skip_group_check=not fin3), deps=vdep)
            ec.tag = "s2.ar"
            if lvl <= 4:
                return
            bar1 = ec.bar()
            ev = []
            dkv_r = []
            for rnd in range(2):
                evr = []
                for b in range(2):
                    evr.append(ec.op("v", lambda v, b=b, rnd=rnd: v.tensor_copy(
                        stage[:, b * 512:(b + 1) * 512], pb[4 + rnd * 2 + b][:, 0:512]),
                        deps=bar1 + dkv_r, psum=4 + rnd * 2 + b))
                dkv_r = [ec.dma("g", "cckv", lambda g, rnd=rnd: g.dma_start(
                    bass.AP(cc_in.tensor, rnd * 128 * 1024, [[1024, 128], [1, 1024]]),
                    stage[:, 0:1024]), deps=evr)]
            dkv1 = dkv_r[0]
            ev.append(ec.op("v", lambda v: v.tensor_copy(smallp[:, 104:136], pb[3][:, 0:32]),
                            deps=bar1, psum=3))
            ev.append(ec.op("v", lambda v: v.tensor_copy(smallp[:, 136:144], pb[3][:, 32:40]),
                            deps=bar1, psum=3))
            dkv2 = ec.dma("g", "cckv", lambda g: g.dma_start(
                bass.AP(cc_in.tensor, 128 * 2048, [[32, 128], [1, 32]]), smallp[:, 104:136]),
                deps=ev + [dkv1])
            mmtv = ec.op("t", lambda te: te.transpose(
                pb[2][0:8, 0:128], smallp[:, 136:144], ident_f[:]),
                deps=ev, wbank=2)
            t_tv = ec.op("v", lambda v: v.tensor_copy(stage[0:8, 0:128], pb[2][0:8, 0:128]),
                         deps=[mmtv, ("D:cckv", ec.dcnt["cckv"])], psum=2)
            dkv3 = ec.dma("g", "cckv", lambda g: g.dma_start(
                bass.AP(cc_in.tensor, 128 * 2048 + 32 * 128, [[128, 8], [1, 128]]),
                stage[0:8, 0:128]), deps=[t_tv])
            NCC = 128 * 2048 + 32 * 128 + 1024
            cckv = ec.cc(lambda g: g.collective_compute(
                "AllReduce", OP.add, replica_groups=groups,
                ins=[bass.AP(cc_in.tensor, 0, [[1, NCC]]).opt()],
                outs=[bass.AP(cc_out.tensor, 0, [[1, NCC]]).opt()]), deps=[dkv1, dkv2, dkv3])
            dr1a = ec.dma("g", "cckv", lambda g: g.dma_start(
                stage[:, 0:1024], bass.AP(cc_out.tensor, 0, [[1024, 128], [1, 1024]])),
                deps=[cckv, ("v", ec.cnt["v"])])
            t_kva = ec.op("v", lambda v: v.tensor_copy(poolC[:, 0:1024], stage[:, 0:1024]),
                          deps=[dr1a, ("t", ec.cnt["t"]), ("s", ec.cnt["s"])])
            dr1 = ec.dma("g", "cckv", lambda g: g.dma_start(
                stage[:, 0:1024], bass.AP(cc_out.tensor, 128 * 1024, [[1024, 128], [1, 1024]])),
                deps=[cckv, t_kva])
            dr2 = ec.dma("g", "cckv", lambda g: g.dma_start(
                smallp[:, 152:184], bass.AP(cc_out.tensor, 128 * 2048, [[32, 128], [1, 32]])),
                deps=[cckv, dr1])
            dr3 = ec.dma("g", "cckv", lambda g: g.dma_start(
                poolC[0:1, 2048:3072],
                bass.AP(cc_out.tensor, 128 * 2048 + 32 * 128, [[1024, 1], [1, 1024]])),
                deps=[cckv, dr2])
            t_kvb = ec.op("v", lambda v: v.tensor_copy(poolC[:, 1024:2048], stage[:, 0:1024]),
                          deps=[dr1, t_kva, ("t", ec.cnt["t"]), ("s", ec.cnt["s"])])
            t_vsb = dr3
            ec.tag = "s2.corr"
            if lvl <= 5:
                return
            t_p1 = ec.op("g", lambda g: g.memset(poolC[0:1, 3072:3200], 0.1),
                         deps=[t_kvb, ("t", ec.cnt["t"]), ("s", ec.cnt["s"])])
            kvq0 = 4096
            bld = []
            prev_ev = {0: None, 1: None}
            for h in range(NH):
                for fc in range(2):
                    sidx = h * 2 + fc
                    bank = pb[sidx % 2]
                    mA = ec.op("t", lambda te, sidx=sidx, bank=bank: te.matmul(
                        bank[:, 0:64], ident_b[:], poolC[:, sidx * 64: sidx * 64 + 64],
                        start=True, stop=False),
                        deps=[t_kvb, t_idb], wbank=sidx % 2)
                    mB = ec.op("t", lambda te, h=h, bank=bank: te.matmul(
                        bank[:, 0:64], poolC[0:1, 3072:3200], poolC[0:1, 2048 + h * 64: 2048 + h * 64 + 64],
                        start=False, stop=True), deps=[t_vsb, t_p1])
                    t_ev = ec.op("s", lambda s, sidx=sidx, bank=bank: s.activation(
                        poolA[:, kvq0 + sidx * 128: kvq0 + sidx * 128 + 64], bank[:, 0:64],
                        AF.Identity, bias=0.0, scale=1.0), deps=[mB, ("t", ec.cnt["t"])], psum=sidx % 2)
                    prev_ev[sidx % 2] = t_ev
                    t_kd = ec.op("v", lambda v, sidx=sidx: v.tensor_scalar(
                        msq[:, 0:1], smallp[:, 152 + sidx:152 + sidx + 1], 0.1 * Tglob, None,
                        OP.add, OP.bypass), deps=[dr2, ("v", ec.cnt["v"])])
                    t_kdb = ec.op("v", lambda v, sidx=sidx: v.tensor_scalar(
                        poolA[:, kvq0 + sidx * 128 + 64: kvq0 + sidx * 128 + 128], ones_b[:, 0:64],
                        msq[0:128, 0:1], None, OP.mult, OP.bypass), deps=[t_kd])
                    bld += [t_ev, t_kdb]
            if int(os.environ.get("K_CORR", "9")) <= 1:
                return
            mmc = None
            for pr in range(PT):
                for which in range(2):
                    for part in range(2):
                        h = pr * 2 + part
                        po = part * 64
                        for fc in range(2):
                            sidx = h * 2 + fc
                            mmc = ec.op("t", lambda te, sidx=sidx, po=po, which=which, fc=fc,
                                        pr=pr: te.matmul(
                                pb[2][po:po + 64, 48 + pr * 2 + which: 48 + pr * 2 + which + 1],
                                poolA[:, kvq0 + sidx * 128 + which * 64: kvq0 + sidx * 128 + which * 64 + 64],
                                ones_b[:, 0:1], start=(fc == 0), stop=(fc == 1)),
                                deps=bld, wbank=2 if pr == 0 and which == 0 and fc == 0 else None)
            t_corr = ec.op("v", lambda v: v.tensor_scalar(
                smallp[:, 200:216], pb[2][:, 48:64], 0.1, None,
                OP.mult, OP.bypass), deps=[mmc, ("v", ec.cnt["v"])], psum=2)
            corr = {(pr, which): t_corr for pr in range(PT) for which in range(2)}
            ec.tag = "s2.ph2"
            if lvl <= 6:
                return
            for tc in range(NTC):
                c0 = tc * TCW
                tdep = [("t", ec.cnt["t"])]
                h_toks = []
                for p in range(PT):
                    th, _ = emit_h_chunk(tc, p, ln_toks + tdep)
                    h_toks.append(th)
                qn_toks = proj_norm_chunk(tc, "Wq", 40, h_toks, d_bq)
                if lvl <= 7:
                    continue
                at_toks = []
                for pr in range(PT):
                    dv = {}
                    for part in range(2):
                        h = pr * 2 + part
                        po = part * 64
                        qp_ev = [None, None]
                        for fc in range(2):
                            mmqp = ec.op("t", lambda te, h=h, po=po, fc=fc: te.matmul(
                                pb[3 + fc][:, 0:TCW],
                                projs[po:po + 64, (h // 2) * NF + fc * 128: (h // 2) * NF + (fc + 1) * 128],
                                poolA[po:po + 64, (h // 2) * 512: (h // 2) * 512 + TCW],
                                start=True, stop=True),
                                deps=[qn_toks[h // 2], dproj], wbank=3 + fc)
                            qp_ev[fc] = ec.op("s", lambda s, fc=fc: s.activation(
                                poolB[:, fc * 512: fc * 512 + TCW], pb[3 + fc][:, 0:TCW],
                                AF.Relu, bias=0.0, scale=1.0),
                                deps=[mmqp, ("t", ec.cnt["t"]), ("v", ec.cnt["v"])], psum=3 + fc)
                        for fc in range(2):
                            sidx = h * 2 + fc
                            stq = fc == 0
                            fq = fc == 1
                            ec.op("t", lambda te, sidx=sidx, po=po, fc=fc, stq=stq, fq=fq: te.matmul(
                                pb[5][po:po + 64, 0:TCW],
                                poolA[:, kvq0 + sidx * 128: kvq0 + sidx * 128 + 64],
                                poolB[:, fc * 512: fc * 512 + TCW], start=stq, stop=fq,
                                skip_group_check=not (stq or fq)),
                                deps=[qp_ev[fc]] + bld, wbank=5 if stq else None)
                            ec.op("t", lambda te, sidx=sidx, po=po, fc=fc, stq=stq, fq=fq: te.matmul(
                                pb[6][po:po + 64, 0:TCW],
                                poolA[:, kvq0 + sidx * 128 + 64: kvq0 + sidx * 128 + 128],
                                poolB[:, fc * 512: fc * 512 + TCW], start=stq, stop=fq,
                                skip_group_check=not (stq or fq)),
                                deps=[qp_ev[fc]], wbank=6 if stq else None)
                    t_nm = ec.op("s", lambda s, pr=pr: s.activation(
                        dbuf[:, 0:TCW], pb[5][:, 0:TCW], AF.Identity,
                        bias=smallp[:, 200 + pr * 2: 200 + pr * 2 + 1], scale=1.0),
                        deps=[("t", ec.cnt["t"]), corr[(pr, 0)], ("v", ec.cnt["v"])], psum=5)
                    t_dn = ec.op("s", lambda s, pr=pr: s.activation(
                        hbuf[:, 0:TCW], pb[6][:, 0:TCW], AF.Ln,
                        bias=smallp[:, 200 + pr * 2 + 1: 200 + pr * 2 + 2], scale=1.0),
                        deps=[("t", ec.cnt["t"]), corr[(pr, 1)]], psum=6)
                    t_dn2 = ec.op("s", lambda s: s.activation(
                        hbuf[:, 0:TCW], hbuf[:, 0:TCW], AF.Exp, bias=0.0, scale=-1.0),
                        deps=[t_dn])
                    at_toks.append(ec.op("v", lambda v, pr=pr: v.tensor_tensor(
                        poolC[:, pr * 512: pr * 512 + TCW], dbuf[:, 0:TCW], hbuf[:, 0:TCW],
                        OP.mult), deps=[t_nm, t_dn2, ("v", ec.cnt["v"])]))
                if lvl <= 8:
                    continue
                for m in range(PT):
                    slot, wtok = ws.load(arena["Wo0"] + m * 128, extra_deps=arena_tok["Wo0"],
                                         colblock=(D, 128))
                    bank = pb[7] if m % 2 == 0 else pb[2]
                    mmo = None
                    for kt in range(PT):
                        mmo = ec.op("t", lambda te, kt=kt, bank=bank, slot=slot: te.matmul(
                            bank[:, 0:TCW], sw[:, slot * D + kt * 128: slot * D + (kt + 1) * 128],
                            poolC[:, kt * 512: kt * 512 + TCW], start=(kt == 0), stop=(kt == PT - 1)),
                            deps=[wtok, at_toks[kt]],
                            wbank=(7 if m % 2 == 0 else 2) if kt == 0 else None)
                    ws.done(slot, mmo)
                    t_u = ec.op("v", lambda v, m=m, bank=bank: v.tensor_scalar(
                        varb[:, 0:TCW], bank[:, 0:TCW], biasp[:, 64 + m:64 + m + 1], None,
                        OP.add, OP.bypass), deps=[mmo, d_bo0, ("v", ec.cnt["v"])],
                        psum=7 if m % 2 == 0 else 2)
                    ec.op("v", lambda v, m=m, c0=c0: v.tensor_tensor(
                        X[:, m * Tc + c0: m * Tc + c0 + TCW],
                        X[:, m * Tc + c0: m * Tc + c0 + TCW], varb[:, 0:TCW], OP.add),
                        deps=[t_u])


        def emit_l1ffn(tag):
            ec.tag = f"s{tag}.pA"
            T_loc = float(Tc)
            ln_toks = emit_ln(ec.bar())
            bar0 = ec.bar()
            d_bb = ec.dma("y", "mbb", lambda y: y.dma_start(biasp[:, 72:80], W["b_base"]), deps=bar0)
            sh = 84
            hw_toks = []
            hwl = {}
            for tc in range(NTC):
                for p in range(PT):
                    th, _ = emit_h_chunk(tc, p, ln_toks + [hwl.get(p)] +
                                         (bar0 if tc == 0 and p == 0 else []))
                    if tc == 0:
                        ec.op("v", lambda v, p=p: v.tensor_scalar(
                            msq[:, 0:TCW], poolB[:, p * 512: p * 512 + TCW], 1.0, 0.0,
                            OP.mult, OP.add, accum_out=smallp[:, sh + p: sh + p + 1]),
                            deps=[th, ("v", ec.cnt["v"])])
                    else:
                        ec.op("v", lambda v, p=p: v.tensor_scalar(
                            msq[:, 0:TCW], poolB[:, p * 512: p * 512 + TCW], 1.0,
                            smallp[0:128, sh + p: sh + p + 1],
                            OP.mult, OP.add, accum_out=smallp[:, sh + p: sh + p + 1]),
                            deps=[th, ("v", ec.cnt["v"])])
                    hw_toks.append(ec.dma("y", f"hs{p}", lambda y, tc=tc, p=p: y.dma_start(
                        bass.AP(h_scr.tensor, p * 128 * Tc + tc * TCW, [[Tc, 128], [1, TCW]]),
                        poolB[:, p * 512: p * 512 + TCW]), deps=[th]))
                    hwl[p] = hw_toks[-1]
            sumh_tok = ("v", ec.cnt["v"])
            ec.tag = f"s{tag}.pB"
            acc_cols = {"a0": 102, "a1": 103, "b0": 104, "b1": 105}
            loss_toks = []
            for e in range(NE):
                ebar = [("t", ec.cnt["t"]), ("s", ec.cnt["s"]), ("v", ec.cnt["v"])]

                d_bd2 = ec.dma("y", "mbd2", lambda y, e=e: y.dma_start(
                    biasp[:, 80:88], bass.AP(W["bd2"].tensor, e * D, [[8, 128], [1, 8]])),
                    deps=ebar)
                gm_mm = {0: None, 1: None}
                mmw = None
                for hf in range(2):
                    d_w2f = ec.dma("y", "wd2f", lambda y, e=e, hf=hf: y.dma_start(
                        stage[:, 0:1024], bass.AP(W["Wd2T"].tensor, e * D * HD1 + hf * 4 * 128 * HD1,
                                          [[HD1, 128], [128 * HD1, 4], [1, HD1]])),
                        deps=hw_toks[-2:] + ebar + [("t", ec.cnt["t"])])
                    for m2 in range(2):
                        for kt4 in range(4):
                            gm_mm[m2] = ec.op("t", lambda te, m2=m2, kt4=kt4, hf=hf: te.matmul(
                                pb[m2][:, 0:256],
                                stage[:, kt4 * 256 + m2 * 128: kt4 * 256 + (m2 + 1) * 128],
                                stage[:, kt4 * 256: (kt4 + 1) * 256],
                                start=(kt4 == 0 and hf == 0), stop=(kt4 == 3 and hf == 1)),
                                deps=[d_w2f] + (ebar if m2 == 0 and kt4 == 0 and hf == 0 else []),
                                wbank=m2 if kt4 == 0 and hf == 0 else None)
                    for kt4 in range(4):
                        mmw = ec.op("t", lambda te, kt4=kt4, hf=hf: te.matmul(
                            pb[2][0:1, 0:256], biasp[:, 80 + hf * 4 + kt4:80 + hf * 4 + kt4 + 1],
                            stage[:, kt4 * 256: (kt4 + 1) * 256],
                            start=(kt4 == 0 and hf == 0), stop=(kt4 == 3 and hf == 1)),
                            deps=[d_w2f, d_bd2],
                            wbank=2 if kt4 == 0 and hf == 0 else None)
                gm_toks = []
                for m2 in range(2):
                    gm_toks.append(ec.op("v", lambda v, m2=m2: v.tensor_copy(
                        gbuf[:, m2 * 256:(m2 + 1) * 256], pb[m2][:, 0:256]),
                        deps=[gm_mm[m2], ("v", ec.cnt["v"])], psum=m2))
                t_w2b = ec.op("v", lambda v: v.tensor_scalar(
                    msq[0:1, 256:512], pb[2][0:1, 0:256], 2.0, None, OP.mult, OP.bypass),
                    deps=[mmw, ("v", ec.cnt["v"])], psum=2)
                t_bsq = ec.op("s", lambda s: s.activation(
                    msq[:, 0:8], biasp[:, 80:88], AF.Square, bias=0.0, scale=1.0),
                    deps=[d_bd2, ("v", ec.cnt["v"]), ("t", ec.cnt["t"])])
                t_bsc = ec.op("v", lambda v: v.tensor_reduce(
                    smallp[:, 100:101], msq[:, 0:8], axis=AX.X, op=OP.add), deps=[t_bsq])
                t_bsh0 = ec.op("v", lambda v: v.tensor_tensor(
                    msq[:, 0:8], biasp[:, 80:88], smallp[:, sh:sh + 8], OP.mult),
                    deps=[d_bd2, sumh_tok, ("v", ec.cnt["v"])])
                t_bsh = ec.op("v", lambda v: v.tensor_scalar(
                    varb[:, 0:8], msq[:, 0:8], 1.0, 0.0, OP.mult, OP.add,
                    accum_out=smallp[:, 101:102]), deps=[t_bsh0])
                last_acc = {}
                for tc in range(NTC):
                    hr_toks = []
                    for p in range(PT):
                        hr_toks.append(ec.dma("y", f"hr{p}", lambda y, tc=tc, p=p: y.dma_start(
                            poolB[:, p * 512: p * 512 + TCW],
                            bass.AP(h_scr.tensor, p * 128 * Tc + tc * TCW, [[Tc, 128], [1, TCW]])),
                            deps=hw_toks + [("t", ec.cnt["t"]), ("v", ec.cnt["v"])]))
                    hv_toks = []
                    for m2 in range(2):
                        slot, wtok = ws.load(arena["Wd1"] + e * D * HD1 + m2 * 128,
                                             extra_deps=arena_tok["Wd1"], colblock=(HD1, 128))
                        mmh = None
                        for kt in range(PT):
                            mmh = ec.op("t", lambda te, kt=kt, m2=m2, slot=slot: te.matmul(
                                pb[m2][:, 0:TCW], sw[:, slot * D + kt * 128: slot * D + (kt + 1) * 128],
                                poolB[:, kt * 512: kt * 512 + TCW], start=(kt == 0), stop=(kt == PT - 1)),
                                deps=[wtok, hr_toks[kt]], wbank=m2 if kt == 0 else None)
                        ws.done(slot, mmh)
                        if tc == 0 and m2 == 0:
                            d_bd1 = ec.dma("y", "mbd1", lambda y, e=e: y.dma_start(
                                biasp[:, 88:90], bass.AP(W["bd1"].tensor, e * 256, [[2, 128], [1, 2]])),
                                deps=ebar)
                            emit_l1ffn.d_bd1 = d_bd1
                        hv_toks.append(ec.op("s", lambda s, m2=m2: s.activation(
                            stage[:, m2 * 512: m2 * 512 + TCW], pb[m2][:, 0:TCW], AF.Relu,
                            bias=biasp[:, 88 + m2:88 + m2 + 1], scale=1.0),
                            deps=[mmh, emit_l1ffn.d_bd1], psum=m2))
                    z_toks = []
                    for m2 in range(2):
                        slot, wtok = ws.load(arena["Wd2T"] + e * D * HD1 + m2 * 128,
                                             extra_deps=arena_tok["Wd2T"], colblock=(HD1, 128))
                        mmz = None
                        for kt in range(PT):
                            mmz = ec.op("t", lambda te, kt=kt, m2=m2, slot=slot: te.matmul(
                                pb[2 + m2][:, 0:TCW], sw[:, slot * D + kt * 128: slot * D + (kt + 1) * 128],
                                poolB[:, kt * 512: kt * 512 + TCW], start=(kt == 0), stop=(kt == PT - 1)),
                                deps=[wtok, hr_toks[kt]], wbank=(2 + m2) if kt == 0 else None)
                        ws.done(slot, mmz)
                        z_toks.append(mmz)
                    q_toks = []
                    for m2 in range(2):
                        mmq = None
                        for k2 in range(2):
                            mmq = ec.op("t", lambda te, m2=m2, k2=k2: te.matmul(
                                pb[4 + m2][:, 0:TCW], gbuf[:, k2 * 256 + m2 * 128: k2 * 256 + (m2 + 1) * 128],
                                stage[:, k2 * 512: k2 * 512 + TCW], start=(k2 == 0), stop=False),
                                deps=[gm_toks[k2], hv_toks[k2]],
                                wbank=(4 + m2) if k2 == 0 else None)
                        for cq in range(TCW // 128):
                            mmq = ec.op("t", lambda te, m2=m2, cq=cq: te.matmul(
                                pb[4 + m2][:, cq * 128:(cq + 1) * 128],
                                msq[0:1, 256 + m2 * 128:256 + (m2 + 1) * 128],
                                ones_f[0:1, 0:128], start=False, stop=(cq == TCW // 128 - 1),
                                skip_group_check=(cq != TCW // 128 - 1)), deps=[t_w2b])
                        q_toks.append(mmq)
                    for m2 in range(2):
                        for nm, bank, tok in (("a", pb[4 + m2], q_toks[m2]), ("b", pb[2 + m2], z_toks[m2])):
                            key = nm + str(m2)
                            col = acc_cols[key]
                            seed = 0.0 if key not in last_acc else smallp[0:128, col + 8:col + 9] \
                                if False else (0.0 if key not in last_acc else smallp[0:128, col:col + 1])
                            dst = smallp[:, col:col + 1]
                            t_pr = ec.op("v", lambda v, m2=m2, bank=bank, nm=nm: v.tensor_tensor(
                                varb[:, 0:TCW], stage[:, m2 * 512: m2 * 512 + TCW], bank[:, 0:TCW],
                                OP.mult), deps=[tok, hv_toks[m2], ("v", ec.cnt["v"])],
                                psum=(4 + m2) if nm == "a" else (2 + m2))
                            if key not in last_acc:
                                t_acc = ec.op("v", lambda v, dst=dst: v.tensor_scalar(
                                    dbuf[:, 0:TCW], varb[:, 0:TCW], 1.0, 0.0, OP.mult, OP.add,
                                    accum_out=dst), deps=[t_pr])
                            else:
                                t_acc = ec.op("v", lambda v, dst=dst, col=col: v.tensor_scalar(
                                    dbuf[:, 0:TCW], varb[:, 0:TCW], 1.0, smallp[0:128, col:col + 1],
                                    OP.mult, OP.add, accum_out=dst),
                                    deps=[t_pr, last_acc[key]])
                            last_acc[key] = t_acc
                t_c1 = ec.op("v", lambda v: v.tensor_tensor(
                    varb[:, 0:1], smallp[:, 102:103], smallp[:, 103:104], OP.add),
                    deps=[last_acc["a0"], last_acc["a1"], ("v", ec.cnt["v"])])
                t_c2 = ec.op("v", lambda v: v.tensor_tensor(
                    varb[:, 1:2], smallp[:, 104:105], smallp[:, 105:106], OP.add),
                    deps=[last_acc["b0"], last_acc["b1"], t_c1])
                t_c3 = ec.op("v", lambda v: v.scalar_tensor_tensor(
                    varb[:, 0:1], varb[:, 1:2], -2.0, varb[:, 0:1], OP.mult, OP.add), deps=[t_c2])
                t_c4 = ec.op("v", lambda v: v.scalar_tensor_tensor(
                    varb[:, 0:1], smallp[:, 100:101], T_loc, varb[:, 0:1], OP.mult, OP.add),
                    deps=[t_c3, t_bsc])
                t_c5 = ec.op("v", lambda v: v.scalar_tensor_tensor(
                    varb[:, 0:1], smallp[:, 101:102], -2.0, varb[:, 0:1], OP.mult, OP.add),
                    deps=[t_c4, t_bsh])
                mml = ec.op("t", lambda te: te.matmul(
                    pb[6][0:1, 0:1], varb[:, 0:1], ones_f[:, 0:1], start=True, stop=True),
                    deps=[t_c5], wbank=6)
                loss_toks.append(ec.op("v", lambda v, e=e: v.tensor_copy(
                    msq[0:1, 232 + e:233 + e], pb[6][0:1, 0:1]), deps=[mml, ("v", ec.cnt["v"])], psum=6))
            ec.tag = f"s{tag}.ar"
            dl1 = ec.dma("g", "ccio", lambda g: g.dma_start(
                bass.AP(agr_in.tensor, 0, [[8, 1], [1, 8]]), msq[0:1, 232:240]), deps=loss_toks)
            ccl = ec.cc(lambda g: g.collective_compute(
                "AllReduce", OP.add, replica_groups=groups,
                ins=[bass.AP(agr_in.tensor, 0, [[1, 8]]).opt()],
                outs=[bass.AP(ag_out.tensor, 0, [[1, 8]]).opt()]), deps=[dl1])
            dl2 = ec.dma("g", "ccio", lambda g: g.dma_start(
                msq[0:1, 240:248], bass.AP(ag_out.tensor, 0, [[8, 1], [1, 8]])), deps=[ccl])
            t_mn = ec.op("v", lambda v: v.tensor_reduce(
                msq[0:1, 208:209], msq[0:1, 240:248], axis=AX.X, op=OP.min), deps=[dl2, ("v", ec.cnt["v"])])
            t_oh = ec.op("v", lambda v: v.tensor_scalar(
                msq[0:1, 216:224], msq[0:1, 240:248], msq[0:1, 208:209], None, OP.is_le, OP.bypass),
                deps=[t_mn])
            t_ix0 = ec.op("v", lambda v: v.tensor_tensor(
                msq[0:1, 216:224], msq[0:1, 216:224], iot8[0:1, 0:8], OP.mult),
                deps=[t_oh, d_ci])
            t_ix = ec.op("v", lambda v: v.tensor_scalar(
                varb[0:1, 0:NE], msq[0:1, 216:224], 1.0, 0.0, OP.mult, OP.add,
                accum_out=msq[0:1, 224:225]), deps=[t_ix0])
            t_ii = ec.op("v", lambda v: v.tensor_copy(idxi[:], msq[0:1, 224:225]), deps=[t_ix])
            ec.raw("g", lambda g: g.reg_load(selreg, idxi[0:1, 0:1]), deps=[t_ii])
            gkey = f"ga{tag}"
            gdeps2 = ec._waits("g", ec.bar())
            if gkey not in ec.dsem:
                ec.dsem[gkey] = es.enter_context(nc.semaphore("dk_" + gkey))
                ec.dcnt[gkey] = 0
            ec.dcnt[gkey] += 64

            def _gad(g, gdeps2=gdeps2):
                for s_, v_ in gdeps2:
                    g.wait_ge(s_, v_)
                for e in range(NE):
                    with g.If_eq(selreg, e):
                        g.dma_start(poolA[:, 1536:1536 + 8 * RAD],
                                    bass.AP(W["Wa_down"].tensor, e * D * RAD,
                                            [[RAD, 128], [128 * RAD, PT], [1, RAD]])
                                    ).then_inc(ec.dsem[gkey], 16)
                        g.dma_start(poolA[0:64, 512:512 + D],
                                    bass.AP(W["Wa_up"].tensor, e * RAD * D, [[D, 64], [1, D]])
                                    ).then_inc(ec.dsem[gkey], 16)
                        g.dma_start(smallp[0:64, 110:111],
                                    bass.AP(W["ba_down"].tensor, e * RAD, [[1, 64], [1, 1]])
                                    ).then_inc(ec.dsem[gkey], 16)
                        g.dma_start(biasp[:, 96:104],
                                    bass.AP(W["ba_up"].tensor, e * D, [[8, 128], [1, 8]])
                                    ).then_inc(ec.dsem[gkey], 16)
            ec.plan["g"].append((ec.tag, _gad))
            gad = ("D:" + gkey, ec.dcnt[gkey])
            t_cb = ec.op("v", lambda v: v.tensor_tensor(
                biasp[:, 88:96], biasp[:, 72:80], biasp[:, 96:104], OP.add),
                deps=[gad, d_bb, ("v", ec.cnt["v"]), ("s", ec.cnt["s"])])
            ec.tag = f"s{tag}.p2"
            for tc in range(NTC):
                c0 = tc * TCW
                hr_toks = []
                for p in range(PT):
                    hr_toks.append(ec.dma("y", f"hr{p}", lambda y, tc=tc, p=p: y.dma_start(
                        poolB[:, p * 512: p * 512 + TCW],
                        bass.AP(h_scr.tensor, p * 128 * Tc + tc * TCW, [[Tc, 128], [1, TCW]])),
                        deps=[("t", ec.cnt["t"]), ("v", ec.cnt["v"])]))
                mma = None
                for kt in range(PT):
                    mma = ec.op("t", lambda te, kt=kt: te.matmul(
                        pb[6][0:64, 0:TCW], poolA[:, 1536 + kt * RAD:1536 + (kt + 1) * RAD],
                        poolB[:, kt * 512: kt * 512 + TCW], start=(kt == 0), stop=(kt == PT - 1)),
                        deps=[gad, hr_toks[kt]], wbank=6 if kt == 0 else None)
                t_a = ec.op("s", lambda s: s.activation(
                    poolA[0:64, 0:TCW], pb[6][0:64, 0:TCW], AF.Relu,
                    bias=smallp[0:64, 110:111], scale=1.0), deps=[mma, gad], psum=6)
                for m in range(PT):
                    slot, wtok = ws.load(arena["W_base"] + m * 128, extra_deps=arena_tok["W_base"],
                                         colblock=(D, 128))
                    bank = pb[7] if m % 2 == 0 else pb[2]
                    mmb = None
                    for kt in range(PT):
                        mmb = ec.op("t", lambda te, kt=kt, bank=bank, slot=slot: te.matmul(
                            bank[:, 0:TCW], sw[:, slot * D + kt * 128: slot * D + (kt + 1) * 128],
                            poolB[:, kt * 512: kt * 512 + TCW], start=(kt == 0), stop=False),
                            deps=[wtok, hr_toks[kt]],
                            wbank=(7 if m % 2 == 0 else 2) if kt == 0 else None)
                    mmb = ec.op("t", lambda te, m=m, bank=bank: te.matmul(
                        bank[:, 0:TCW], poolA[0:64, 512 + m * 128:512 + (m + 1) * 128],
                        poolA[0:64, 0:TCW], start=False, stop=True), deps=[t_a, gad])
                    ws.done(slot, mmb)
                    t_u = ec.op("v", lambda v, m=m, bank=bank: v.tensor_scalar(
                        varb[:, 0:TCW], bank[:, 0:TCW], biasp[:, 88 + m:88 + m + 1], 0.5,
                        OP.add, OP.mult), deps=[mmb, t_cb, ("v", ec.cnt["v"])],
                        psum=7 if m % 2 == 0 else 2)
                    ec.op("v", lambda v, m=m, c0=c0: v.tensor_tensor(
                        X[:, m * Tc + c0: m * Tc + c0 + TCW],
                        X[:, m * Tc + c0: m * Tc + c0 + TCW], varb[:, 0:TCW], OP.add),
                        deps=[t_u])


        def emit_l1attn():
            ec.tag = "s5.kt"
            def emit_rho_chunk(c0, ncols, deps):
                t1 = ec.op("s", lambda s: s.activation(
                    msq[:, 0:ncols], rstd_t[:, c0:c0 + ncols], AF.Square, bias=0.0, scale=1.0),
                    deps=list(deps) + [("s", ec.cnt["s"]), ("v", ec.cnt["v"])])
                t2 = ec.op("v", lambda v: v.tensor_scalar(
                    msq[:, 0:ncols], msq[:, 0:ncols], -1e-5, 1.0 + 1e-5, OP.mult, OP.add),
                    deps=[t1])
                t3 = ec.op("s", lambda s: s.activation(
                    msq[:, 0:ncols], msq[:, 0:ncols], AF.Ln, bias=0.0, scale=1.0), deps=[t2])
                t4 = ec.op("s", lambda s: s.activation(
                    msq[:, 0:ncols], msq[:, 0:ncols], AF.Exp, bias=0.0, scale=-0.5), deps=[t3])
                return ec.op("v", lambda v, c0=c0: v.tensor_tensor(
                    varb[:, 0:ncols], msq[:, 0:ncols], rstd_t[:, c0:c0 + ncols], OP.mult),
                    deps=[t4, ("v", ec.cnt["v"])])

            ln_toks = emit_ln(ec.bar())
            bar0 = ec.bar()
            YS = D * Tc
            d_lnw = ec.dma("y", "mlnw", lambda y: y.dma_start(biasp[:, 40:48], W["ln_w"]), deps=bar0)
            d_lnb = ec.dma("y", "mlnb", lambda y: y.dma_start(biasp[:, 48:56], W["ln_b"]), deps=bar0)
            d_dv = ec.dma("y", "mdv", lambda y: y.dma_start(biasp[:, 56:64], W["D_vec"]), deps=bar0)
            d_ob = ec.dma("y", "mob", lambda y: y.dma_start(biasp[:, 64:72], W["out_b"]), deps=bar0)
            d_l2w = ec.dma("y", "ml2w", lambda y: y.dma_start(biasp[:, 72:80], W["ln2_w"]), deps=bar0)
            d_l2b = ec.dma("y", "ml2b", lambda y: y.dma_start(biasp[:, 80:88], W["ln2_b"]), deps=bar0)
            d_f2b = ec.dma("y", "mf2b", lambda y: y.dma_start(biasp[:, 88:96], W["ffn2_b"]), deps=bar0)
            d_f1b = ec.dma("y", "mf1b", lambda y: y.dma_start(
                smallp[:, 112:144], W["ffn1_b"]), deps=bar0)
            d_al = ec.dma("y", "mal", lambda y: y.dma_start(
                stage[:, 0:128], bass.AP(W["A_log"].tensor, 0, [[16, 128], [2048, PT], [1, 16]])),
                deps=bar0)
            d_bm = ec.dma("y", "mbm", lambda y: y.dma_start(
                stage[:, 128:256], bass.AP(W["B_mat"].tensor, 0, [[16, 128], [2048, PT], [1, 16]])),
                deps=bar0)
            d_cm = ec.dma("y", "mcm", lambda y: y.dma_start(
                stage[:, 256:384], bass.AP(W["C_mat"].tensor, 0, [[16, 128], [2048, PT], [1, 16]])),
                deps=bar0)
            t_A = ec.op("s", lambda s: s.activation(stage[:, 0:128], stage[:, 0:128], AF.Sigmoid,
                        bias=0.0, scale=1.0), deps=[d_al, ("v", ec.cnt["v"]), ("t", ec.cnt["t"])])
            t_P = ec.op("v", lambda v: v.tensor_tensor(
                stage[:, 128:256], stage[:, 128:256], stage[:, 256:384], OP.mult),
                deps=[d_bm, d_cm, ("v", ec.cnt["v"])])
            kt_toks = []
            prev = t_P
            for dl in range(K_TAPS):
                for p in range(PT):
                    kt_toks.append(ec.op("v", lambda v, dl=dl, p=p: v.tensor_reduce(
                        gbuf[:, p * K_TAPS + dl: p * K_TAPS + dl + 1],
                        stage[:, 128 + p * 16:128 + (p + 1) * 16], axis=AX.X, op=OP.add),
                        deps=[prev]))
                if dl < K_TAPS - 1:
                    prev = ec.op("v", lambda v: v.tensor_tensor(
                        stage[:, 128:256], stage[:, 128:256], stage[:, 0:128], OP.mult),
                        deps=[t_A] + kt_toks[-PT:])
            kt_done = ("v", ec.cnt["v"])
            ec.tag = "s5.halo"
            tl_toks = []
            c0t = Tc - 32
            t_rhoT = emit_rho_chunk(c0t, 32, bar0)
            t_rc = ec.op("v", lambda v: v.tensor_copy(hbuf[:, 64:96], varb[:, 0:32]), deps=[t_rhoT])
            for p in range(PT):
                t_d = ec.op("v", lambda v, p=p: v.tensor_tensor(
                    dbuf[:, 0:32], X[:, p * Tc + c0t: p * Tc + Tc], mean_t[:, c0t:Tc], OP.subtract),
                    deps=bar0 + [("v", ec.cnt["v"])])
                t_m = ec.op("v", lambda v, p=p: v.tensor_tensor(
                    hbuf[:, 0:32], dbuf[:, 0:32], hbuf[:, 64:96], OP.mult), deps=[t_d, t_rc])
                tl_toks.append(ec.op("v", lambda v, p=p: v.tensor_scalar(
                    poolB[:, 3840 + p * 32:3840 + (p + 1) * 32], hbuf[:, 0:32], biasp[0:128, 40 + p:41 + p],
                    biasp[0:128, 48 + p:49 + p], OP.mult, OP.add), deps=[t_m, d_lnw, d_lnb]))
            d_h1 = ec.dma("g", "hal", lambda g: g.dma_start(
                bass.AP(hal_in.tensor, 0, [[PT * 32, 128], [1, PT * 32]]), poolB[:, 3840:3840 + 256]), deps=tl_toks)
            ccha = ec.cc(lambda g: g.collective_compute(
                "AllGather", OP.bypass, replica_groups=groups,
                ins=[bass.AP(hal_in.tensor, 0, [[1, D * 32]]).opt()],
                outs=[bass.AP(hal_out.tensor, 0, [[1, gsz * D * 32]]).opt()]), deps=[d_h1])
            hs_toks = []
            for k in range(gsz):
                hs_toks.append(ec.dma("g", "hal", lambda g, k=k: g.dma_start(
                    stage[:, k * 256:(k + 1) * 256],
                    bass.AP(hal_out.tensor, k * D * 32, [[PT * 32, 128], [1, PT * 32]])),
                    deps=[ccha, ("s", ec.cnt["s"]), ("t", ec.cnt["t"]), ("v", ec.cnt["v"])]))
            t_hc = ec.op("v", lambda v: v.tensor_scalar(
                poolB[:, 3840:3840 + 256], stage[:, 0:256], biasp[0:128, 104:105], None, OP.mult, OP.bypass),
                deps=hs_toks + [d_cm, ("v", ec.cnt["v"])])
            for k in range(1, gsz):
                t_hc = ec.op("v", lambda v, k=k: v.scalar_tensor_tensor(
                    poolB[:, 3840:3840 + 256], stage[:, k * 256:(k + 1) * 256], biasp[0:128, 104 + k:105 + k],
                    poolB[:, 3840:3840 + 256], OP.mult, OP.add), deps=[t_hc, hs_toks[k]])
            ec.tag = "s5.conv"
            XN0 = 32
            ys_last = {0: None, 1: None}
            for p in range(PT):
                pdep = [("t", ec.cnt["t"]), ("s", ec.cnt["s"])]
                t_hcp = ec.op("v", lambda v, p=p: v.tensor_copy(
                    poolA[:, 0:32], poolB[:, 3840 + p * 32:3840 + (p + 1) * 32]), deps=[t_hc] + pdep +
                    [("v", ec.cnt["v"])])
                xn_toks = []
                for tc in range(NTC):
                    c0 = tc * TCW
                    t_rho = emit_rho_chunk(c0, TCW, ln_toks + pdep)
                    t_d = ec.op("v", lambda v, p=p, c0=c0: v.tensor_tensor(
                        dbuf[:, 0:TCW], X[:, p * Tc + c0: p * Tc + c0 + TCW],
                        mean_t[:, c0:c0 + TCW], OP.subtract), deps=ln_toks + pdep + [("v", ec.cnt["v"])])
                    t_m = ec.op("v", lambda v, c0=c0: v.tensor_tensor(
                        hbuf[:, 0:TCW], dbuf[:, 0:TCW], varb[:, 0:TCW], OP.mult), deps=[t_d, t_rho])
                    xn_toks.append(ec.op("v", lambda v, p=p, c0=c0: v.tensor_scalar(
                        poolA[:, XN0 + c0: XN0 + c0 + TCW], hbuf[:, 0:TCW],
                        biasp[0:128, 40 + p:41 + p], biasp[0:128, 48 + p:49 + p],
                        OP.mult, OP.add), deps=[t_m, d_lnw, d_lnb]))
                dg_toks = []
                for dl in range(K_TAPS):
                    dg_toks.append(ec.op("v", lambda v, dl=dl, p=p: v.tensor_scalar(
                        poolC[:, dl * 128:(dl + 1) * 128], ident_b[:],
                        gbuf[0:128, p * K_TAPS + dl: p * K_TAPS + dl + 1], None,
                        OP.mult, OP.bypass), deps=[kt_done, t_idb] + pdep +
                        ([("v", ec.cnt["v"])] if dl == 0 else [])))
                for tc in range(NTC):
                    c0 = tc * TCW
                    bank = pb[tc % 2]
                    mmc = None
                    for dl in range(K_TAPS):
                        mmc = ec.op("t", lambda te, dl=dl, c0=c0, bank=bank: te.matmul(
                            bank[:, 0:TCW], poolC[:, dl * 128:(dl + 1) * 128],
                            poolA[:, XN0 + c0 - dl: XN0 + c0 - dl + TCW],
                            start=(dl == 0), stop=(dl == K_TAPS - 1)),
                            deps=[dg_toks[dl], xn_toks[tc]] + ([t_hcp] if tc == 0 else []),
                            wbank=(tc % 2) if dl == 0 else None)
                    t_y = ec.op("v", lambda v, p=p, c0=c0, bank=bank: v.scalar_tensor_tensor(
                        varb[:, 0:TCW], poolA[:, XN0 + c0: XN0 + c0 + TCW],
                        biasp[0:128, 56 + p:57 + p], bank[:, 0:TCW], OP.mult, OP.add),
                        deps=[mmc, d_dv, ("v", ec.cnt["v"])], psum=tc % 2)
                    t_sl = ec.op("s", lambda s: s.activation(
                        msq[:, 0:TCW], varb[:, 0:TCW], AF.Silu, bias=0.0, scale=1.0),
                        deps=[t_y, ("s", ec.cnt["s"])])
                    slotc = tc % 2
                    t_yb = ec.op("v", lambda v, slotc=slotc: v.tensor_copy(
                        poolB[:, slotc * 512: slotc * 512 + TCW], msq[:, 0:TCW]),
                        deps=[t_sl, ys_last[slotc], ("v", ec.cnt["v"])])
                    ys_last[slotc] = ec.dma("y", f"ys{slotc}", lambda y, p=p, c0=c0, slotc=slotc:
                                            y.dma_start(
                        bass.AP(h_scr.tensor, YS + p * 128 * Tc + c0, [[Tc, 128], [1, TCW]]),
                        poolB[:, slotc * 512: slotc * 512 + TCW]), deps=[t_yb])
            ec.tag = "s5.yproj"
            stat_first = [True]
            x1_last = {}
            for tc in range(NTC):
                c0 = tc * TCW
                tdep = [("t", ec.cnt["t"]), ("s", ec.cnt["s"])]
                yr_toks = []
                for p in range(PT):
                    yr_toks.append(ec.dma("y", f"yr{p}", lambda y, p=p, c0=c0: y.dma_start(
                        poolB[:, p * 512: p * 512 + TCW],
                        bass.AP(h_scr.tensor, YS + p * 128 * Tc + c0, [[Tc, 128], [1, TCW]])),
                        deps=[ys_last[0], ys_last[1], ("t", ec.cnt["t"]), ("v", ec.cnt["v"])]))
                h_toks = []
                for p in range(PT):
                    th, _ = emit_h_chunk(tc, p, ln_toks + tdep, add_to_x=1.0,
                                         dst=poolA, dst_col=p * 512)
                    h_toks.append(th)
                for m in range(PT):
                    slot, wtok = ws.load(arena["out_wT"] + m * 128, extra_deps=arena_tok["out_wT"],
                                         colblock=(D, 128))
                    bank = pb[2 + m % 2]
                    mmo = None
                    for kt in range(PT):
                        mmo = ec.op("t", lambda te, kt=kt, bank=bank, slot=slot: te.matmul(
                            bank[:, 0:TCW], sw[:, slot * D + kt * 128: slot * D + (kt + 1) * 128],
                            poolB[:, kt * 512: kt * 512 + TCW], start=(kt == 0), stop=(kt == PT - 1)),
                            deps=[wtok, yr_toks[kt]],
                            wbank=(2 + m % 2) if kt == 0 else None)
                    ws.done(slot, mmo)
                    t_yp = ec.op("v", lambda v, m=m, bank=bank: v.tensor_scalar(
                        varb[:, 0:TCW], bank[:, 0:TCW], biasp[:, 64 + m:64 + m + 1], None,
                        OP.add, OP.bypass), deps=[mmo, d_ob, ("v", ec.cnt["v"])], psum=2 + m % 2)
                    ec.op("v", lambda v, m=m, c0=c0: v.tensor_tensor(
                        X[:, m * Tc + c0: m * Tc + c0 + TCW],
                        X[:, m * Tc + c0: m * Tc + c0 + TCW], varb[:, 0:TCW], OP.add),
                        deps=[t_yp])
                    t_x1 = ec.op("v", lambda v, m=m: v.tensor_tensor(
                        poolC[:, m * 512: m * 512 + TCW], varb[:, 0:TCW],
                        poolA[:, m * 512: m * 512 + TCW], OP.add),
                        deps=[t_yp, h_toks[m], x1_last.get(m)])
                    x1_last[m] = ec.dma("y", f"x1{m}", lambda y, m=m, c0=c0: y.dma_start(
                        bass.AP(h_scr.tensor, m * 128 * Tc + c0, [[Tc, 128], [1, TCW]]),
                        poolC[:, m * 512: m * 512 + TCW]), deps=[t_x1])
                    mmsa = ec.op("t", lambda te, m=m: te.matmul(
                        pb[4][:, 0:TCW], ones_b[:, 0:128], poolC[:, m * 512: m * 512 + TCW],
                        start=(m == 0), stop=(m == PT - 1)),
                        deps=[t_x1], wbank=4 if m == 0 else None)
                    t_sq = ec.op("s", lambda s, m=m: s.activation(
                        poolA[:, 4096 + 512: 4096 + 512 + TCW], poolC[:, m * 512: m * 512 + TCW],
                        AF.Square, bias=0.0, scale=1.0),
                        deps=[t_x1, ("s", ec.cnt["s"]), ("t", ec.cnt["t"])])
                    mmsb = ec.op("t", lambda te, m=m: te.matmul(
                        pb[5][:, 0:TCW], ones_b[:, 0:128], poolA[:, 4096 + 512: 4096 + 512 + TCW],
                        start=(m == 0), stop=(m == PT - 1)), deps=[t_sq],
                        wbank=5 if m == 0 else None)
                t_mean2 = ec.op("v", lambda v, c0=c0: v.tensor_scalar(
                    mean_t[:, c0:c0 + TCW], pb[4][:, 0:TCW], 1.0 / D, None, OP.mult, OP.bypass),
                    deps=[mmsa, ("v", ec.cnt["v"])], psum=4)
                t_msq2 = ec.op("s", lambda s: s.activation(
                    msq[:, 0:TCW], pb[4][:, 0:TCW], AF.Square, bias=0.0, scale=1.0 / D),
                    deps=[mmsa, ("s", ec.cnt["s"])], psum=4)
                t_var2 = ec.op("v", lambda v: v.scalar_tensor_tensor(
                    varb[:, 0:TCW], pb[5][:, 0:TCW], 1.0 / D, msq[:, 0:TCW], OP.mult, OP.subtract),
                    deps=[mmsb, t_msq2, t_mean2], psum=5)
                t_ln2 = ec.op("s", lambda s: s.activation(
                    msq[:, 0:TCW], varb[:, 0:TCW], AF.Ln, bias=eps_t[:], scale=1.0), deps=[t_var2])
                ec.op("s", lambda s, c0=c0: s.activation(
                    rstd_t[:, c0:c0 + TCW], msq[:, 0:TCW], AF.Exp, bias=0.0, scale=-0.5),
                    deps=[t_ln2])
            ec.tag = "s5.ffn"
            bar2 = ec.bar()
            for tc in range(NTC):
                c0 = tc * TCW
                x1r = []
                for p in range(PT):
                    x1r.append(ec.dma("y", f"xr{p}", lambda y, p=p, c0=c0: y.dma_start(
                        poolB[:, p * 512: p * 512 + TCW],
                        bass.AP(h_scr.tensor, p * 128 * Tc + c0, [[Tc, 128], [1, TCW]])),
                        deps=[x1_last[p], ("t", ec.cnt["t"]), ("v", ec.cnt["v"])]))
                xn2_toks = []
                for p in range(PT):
                    t_d = ec.op("v", lambda v, p=p, c0=c0: v.tensor_tensor(
                        dbuf[:, 0:TCW], poolB[:, p * 512: p * 512 + TCW], mean_t[:, c0:c0 + TCW],
                        OP.subtract), deps=[x1r[p], ("v", ec.cnt["v"])] + bar2)
                    t_m = ec.op("v", lambda v, c0=c0: v.tensor_tensor(
                        hbuf[:, 0:TCW], dbuf[:, 0:TCW], rstd_t[:, c0:c0 + TCW], OP.mult), deps=[t_d])
                    xn2_toks.append(ec.op("v", lambda v, p=p: v.tensor_scalar(
                        poolA[:, p * 512: p * 512 + TCW], hbuf[:, 0:TCW],
                        biasp[0:128, 72 + p:73 + p], biasp[0:128, 80 + p:81 + p],
                        OP.mult, OP.add), deps=[t_m, d_l2w, d_l2b]))
                for ts_ in range(NKW):
                    tcol = ts_ * 128
                    hdn_toks = []
                    for mh in range(H1 // 128):
                        slot, wtok = ws.load(arena["ffn1_wT"] + mh * 128,
                                             extra_deps=arena_tok["ffn1_wT"], colblock=(H1, 128))
                        bank = pb[mh % 2]
                        mmh = None
                        for kt in range(PT):
                            mmh = ec.op("t", lambda te, kt=kt, bank=bank, slot=slot, tcol=tcol: te.matmul(
                                bank[:, 0:128], sw[:, slot * D + kt * 128: slot * D + (kt + 1) * 128],
                                poolA[:, kt * 512 + tcol: kt * 512 + tcol + 128],
                                start=(kt == 0), stop=(kt == PT - 1)),
                                deps=[wtok, xn2_toks[kt]], wbank=(mh % 2) if kt == 0 else None)
                        ws.done(slot, mmh)
                        hdn_toks.append(ec.op("s", lambda s, mh=mh, bank=bank: s.activation(
                            poolC[:, mh * 128:(mh + 1) * 128], bank[:, 0:128], AF.Gelu,
                            bias=smallp[:, 112 + mh:112 + mh + 1], scale=1.0),
                            deps=[mmh, d_f1b], psum=mh % 2))
                    mm2_last = [None] * PT
                    for kh in range(H1 // 128):
                        slot, wtok = ws.load(arena["ffn2_wT"] + kh * 128 * D,
                                             extra_deps=arena_tok["ffn2_wT"])
                        for mo in range(PT):
                            mm2_last[mo] = ec.op("t", lambda te, kh=kh, mo=mo, slot=slot: te.matmul(
                                pb[mo][:, 128:256], sw[:, slot * D + mo * 128: slot * D + (mo + 1) * 128],
                                poolC[:, kh * 128:(kh + 1) * 128],
                                start=(kh == 0), stop=(kh == H1 // 128 - 1)),
                                deps=[wtok, hdn_toks[kh]], wbank=mo if kh == 0 else None)
                        ws.done(slot, mm2_last[PT - 1])
                    for mo in range(PT):
                        t_u = ec.op("v", lambda v, mo=mo: v.tensor_scalar(
                            varb[:, 0:128], pb[mo][:, 128:256], biasp[:, 88 + mo:88 + mo + 1], None,
                            OP.add, OP.bypass), deps=[mm2_last[mo], d_f2b, ("v", ec.cnt["v"])], psum=mo)
                        ec.op("v", lambda v, mo=mo, c0=c0, tcol=tcol: v.tensor_tensor(
                            X[:, mo * Tc + c0 + tcol: mo * Tc + c0 + tcol + 128],
                            X[:, mo * Tc + c0 + tcol: mo * Tc + c0 + tcol + 128],
                            varb[:, 0:128], OP.add), deps=[t_u])
        if 1 in enable:
            emit_l0ffn(1)
        if 2 in enable:
            emit_l0attn()
        if 3 in enable:
            emit_l0ffn(3)
        if 4 in enable:
            emit_l1ffn(4)
        if 5 in enable:
            emit_l1attn()
        if 6 in enable:
            emit_l1ffn(6)

        ec.tag = "io_out"
        barf = ec.bar()
        st_toks = []
        bank_rd2 = {}
        stage_st = None
        for t in range(NT128 if not os.environ.get("K_NOOUT") else 0):
            stg = stage[:, 0:D]
            cp = []
            for p in range(PT):
                bi = 4 + p % 4
                bank = pb[bi]
                ttok = ec.op("t", lambda te, p=p, t=t, bank=bank: te.transpose(
                    bank[:, 0:128], X[:, p * Tc + t * 128: p * Tc + (t + 1) * 128], ident_f[:]),
                    deps=barf, wbank=bi)
                cp.append(ec.op("v", lambda v, p=p, bank=bank, stg=stg: v.tensor_copy(
                    stg[:, p * 128:(p + 1) * 128], bank[:, 0:128]),
                    deps=[ttok, stage_st], psum=bi))
                bank_rd2[bi] = cp[-1]
            st_toks.append(ec.dma("y", "so0", lambda y, stg=stg, t=t: y.dma_start(
                y_out[t * 128:(t + 1) * 128, :], stg), deps=cp))
            stage_st = st_toks[-1]
        ec.raw("y", lambda y: y.nop(), deps=st_toks)

        for e, attr in (("g", "gpsimd"), ("t", "tensor"), ("v", "vector"), ("s", "scalar"), ("y", "sync")):
            steps = ec.plan[e]

            def runner(eng, steps=steps):
                for tag, fstep in steps:
                    with nc.named_scope(tag):
                        fstep(eng)
            getattr(block, attr)(runner)

    return nc


def prep_inputs(inputs, Tc, n_cores=8):
    C = np.ascontiguousarray

    def f32(a):
        return np.asarray(a, dtype=np.float32)
    x = f32(inputs["L0_attn_x"])
    def vT(a):
        a = np.asarray(a)
        return a.reshape(a.shape[:-1] + (a.shape[-1] // 128, 128)).swapaxes(-1, -2)
    w = {
        "Wq": inputs["L0_attn_Wq"], "bq": vT(inputs["L0_attn_bq"]),
        "Wk": inputs["L0_attn_Wk"], "bk": vT(inputs["L0_attn_bk"]),
        "Wv": inputs["L0_attn_Wv"], "bv": vT(inputs["L0_attn_bv"]),
        "bvR": np.asarray(inputs["L0_attn_bv"]).reshape(1, -1),
        "Wo0": inputs["L0_attn_Wo"], "bo0": vT(inputs["L0_attn_bo"]),
        "projT": np.transpose(np.asarray(inputs["L0_attn_projection"]), (0, 2, 1)),
        "Wp1": inputs["L0_ffn_Wp1"], "bp1": inputs["L0_ffn_bp1"],
        "Wp2": inputs["L0_ffn_Wp2"], "bp2": inputs["L0_ffn_bp2"],
        "W1": inputs["L0_ffn_W1"], "b1T": vT(inputs["L0_ffn_b1"]),
        "W2": inputs["L0_ffn_W2"], "b2T": vT(inputs["L0_ffn_b2"]),
        "Wof": inputs["L0_ffn_Wo"], "bof": vT(inputs["L0_ffn_bo"]),
        "ln_w": vT(inputs["L1_attn_ln_w"]), "ln_b": vT(inputs["L1_attn_ln_b"]),
        "A_log": inputs["L1_attn_A_log"], "B_mat": inputs["L1_attn_B_mat"],
        "C_mat": inputs["L1_attn_C_mat"], "D_vec": vT(inputs["L1_attn_D_vec"]),
        "out_wT": np.asarray(inputs["L1_attn_out_w"]).T, "out_b": vT(inputs["L1_attn_out_b"]),
        "ln2_w": vT(inputs["L1_attn_ln2_w"]), "ln2_b": vT(inputs["L1_attn_ln2_b"]),
        "ffn1_wT": np.asarray(inputs["L1_attn_ffn1_w"]).T, "ffn1_b": vT(inputs["L1_attn_ffn1_b"]),
        "ffn2_wT": np.asarray(inputs["L1_attn_ffn2_w"]).T, "ffn2_b": vT(inputs["L1_attn_ffn2_b"]),
        "W_base": inputs["L1_ffn_W_base"], "b_base": vT(inputs["L1_ffn_b_base"]),
        "Wd1": inputs["L1_ffn_Wd1"], "bd1": vT(inputs["L1_ffn_bd1"]),
        "Wd2T": np.transpose(np.asarray(inputs["L1_ffn_Wd2"]), (0, 2, 1)),
        "bd2": vT(inputs["L1_ffn_bd2"]),
        "Wa_down": inputs["L1_ffn_Wa_down"], "ba_down": inputs["L1_ffn_ba_down"],
        "Wa_up": inputs["L1_ffn_Wa_up"], "ba_up": vT(inputs["L1_ffn_ba_up"]),
    }
    w = {k: C(f32(v)) for k, v in w.items()}
    in_maps = []
    for c in range(n_cores):
        b, q = c // (n_cores // 2), c % (n_cores // 2)
        m = dict(w)
        m["x_in"] = C(x[b, q * Tc:(q + 1) * Tc, :])
        cst = np.zeros((128, 160), np.float32)
        cst[:, 0:128] = np.eye(128, dtype=np.float32)
        cst[0, 128:136] = np.arange(8)
        if q > 0:
            cst[:, 141 + q - 1] = 1.0
        cst[0, 140] = q
        m["consts"] = cst
        in_maps.append(m)
    return in_maps


def kernel(**inputs):
    x = np.asarray(inputs["L0_attn_x"])
    B, S, _ = x.shape
    Tc = S // 4
    in_maps = prep_inputs(inputs, Tc)
    nc = build_program(Tc)
    res = run_bass_kernel_spmd(nc, in_maps, core_ids=list(range(8)))
    outs = []
    for b in range(B):
        outs.append(np.concatenate([res.results[b * 4 + q]["y_out"] for q in range(4)], axis=0))
    return np.stack(outs, axis=0).astype(np.float32)
